# version 11
# speedup vs baseline: 1.2774x; 1.0656x over previous
import sys
if "/opt/trn_rl_repo" not in sys.path:
    sys.path.insert(0, "/opt/trn_rl_repo")
from contextlib import ExitStack

import numpy as np
import ml_dtypes
import concourse.bass as bass
import concourse.mybir as mybir
from concourse.alu_op_type import AluOpType as OP
from concourse.bass_utils import run_bass_kernel_spmd

F32 = mybir.dt.float32
BF16 = mybir.dt.bfloat16
I32 = mybir.dt.int32
AF = mybir.ActivationFunctionType
AX = mybir.AxisListType.X
BF = ml_dtypes.bfloat16

T = 4096
D = 2048
CH = 512
NCH = T // CH
RC = 16
NRC = CH // RC
NPAIR = 120
NH = 32; HK = 32; HV = 64; DK = 1024; DV = 2048
GN_EPS = 1e-5; LN_EPS = 1e-5


def _pairs():
    return [(i, s) for i in range(RC) for s in range(i)]

def host_constants():
    c = {}
    pairs = _pairs()
    Mp = np.zeros((RC, NPAIR), np.float32)
    for p, (i, s) in enumerate(pairs):
        for u in range(s + 1, i):
            Mp[u, p] = -1.0
    c["Mpair"] = Mp.astype(BF)
    Pr = np.zeros((RC, NPAIR), np.float32)
    Pc = np.zeros((RC, NPAIR), np.float32)
    for p, (i, s) in enumerate(pairs):
        Pr[i, p] = 1.0
        Pc[s, p] = 1.0
    c["Prow"] = Pr.astype(BF)
    c["Pcol"] = Pc.astype(BF)
    shifts = np.zeros((8, NPAIR, 128), np.float32)
    for j in range(8):
        for p, (i, s) in enumerate(pairs):
            shifts[j, p, 16 * j + i] = 1.0
    c["ProwShift"] = shifts.astype(BF)
    Mt = np.zeros((RC, RC), np.float32)
    for s in range(RC):
        for u in range(s + 1, RC):
            Mt[u, s] = -1.0
    c["Mtail"] = Mt.astype(BF)
    c["ident"] = np.eye(128, dtype=np.float32).astype(BF)
    onz = np.zeros((128, 128), np.float32)
    onz[0, :] = 1.0; onz[32, :] = 1.0; onz[64, :] = 1.0
    c["ones_row_f32"] = onz
    return c

def pi_perm(n):
    idx = np.arange(n)
    blk = idx // 128
    r = idx % 128
    nat = np.where(r < 64, 2 * r, 2 * r - 127)
    return (128 * blk + nat).astype(np.int64)

def host_prep(inputs):
    g = {}
    f32 = lambda a: np.ascontiguousarray(np.asarray(a), dtype=np.float32)
    g["Wx1"] = f32(inputs["L0_attn_Wx1"])
    Wx2 = f32(inputs["L0_attn_Wx2"])
    g["Wx2A"] = np.ascontiguousarray(Wx2[:4].reshape(128, 2048))
    g["Wx2B"] = np.ascontiguousarray(Wx2[4])
    g["Wr"] = f32(inputs["L0_attn_Wr"])
    g["Ww1"] = f32(inputs["L0_attn_Ww1"])
    g["Ww2"] = f32(inputs["L0_attn_Ww2"])
    g["Wk"] = f32(inputs["L0_attn_Wk"])
    g["Wv"] = f32(inputs["L0_attn_Wv"])
    g["Wg"] = f32(inputs["L0_attn_Wg"])
    g["Wo"] = f32(inputs["L0_attn_Wo"])
    g["mu_cols"] = np.ascontiguousarray(f32(inputs["L0_attn_mu_x"]).reshape(16, 128).T)
    g["bw_cols"] = np.ascontiguousarray(f32(inputs["L0_attn_bw"]).reshape(8, 128).T)
    g["bw_row"] = f32(inputs["L0_attn_bw"]).reshape(1, 1024)
    g["u128"] = np.ascontiguousarray(np.tile(f32(inputs["L0_attn_bonus"]).reshape(1, 1024), (128, 1)))
    g["gnw_t"] = np.ascontiguousarray(np.tile(f32(inputs["L0_attn_gn_w"]).reshape(1, 64), (128, 32)))
    g["gnb_t"] = np.ascontiguousarray(np.tile(f32(inputs["L0_attn_gn_b"]).reshape(1, 64), (128, 32)))
    A = f32(inputs["L0_ffn_A"])
    g["Aff"] = np.ascontiguousarray(A.transpose(2, 1, 0).reshape(2048, 16 * 64))
    Bf = f32(inputs["L0_ffn_B"])
    g["Bff"] = np.ascontiguousarray(Bf.transpose(2, 0, 1).reshape(16 * 64, 2048))
    g["Wff"] = np.ascontiguousarray(f32(inputs["L0_ffn_W"]).T)
    g["bff_row"] = f32(inputs["L0_ffn_bias"]).reshape(1, 2048)
    permD = pi_perm(2048)
    wqkv = f32(inputs["L1_attn_w_qkv"])
    g["Wqkv"] = np.ascontiguousarray(wqkv[permD, :])
    bqkv = f32(inputs["L1_attn_b_qkv"])
    g["bq_cols"] = np.ascontiguousarray(bqkv[0:2048].reshape(16, 128).T)
    g["bk_cols"] = np.ascontiguousarray(bqkv[2048:4096].reshape(16, 128).T)
    g["bv_row"] = bqkv[4096:6144].reshape(1, 2048)
    g["Wout"] = f32(inputs["L1_attn_w_out"])
    g["bout_row"] = f32(inputs["L1_attn_b_out"]).reshape(1, 2048)
    permN = pi_perm(16384)
    g["Pg"] = np.ascontiguousarray(np.asarray(inputs["L1_ffn_gate_W_packed"]).T[:, permN])
    g["Pu"] = np.ascontiguousarray(np.asarray(inputs["L1_ffn_up_W_packed"]).T[:, permN])
    g["Pd"] = np.ascontiguousarray(np.asarray(inputs["L1_ffn_down_W_packed"]).T)
    g["sgT"] = np.ascontiguousarray(f32(inputs["L1_ffn_gate_scales"]).T[:, permN])
    g["suT"] = np.ascontiguousarray(f32(inputs["L1_ffn_up_scales"]).T[:, permN])
    g["sdT"] = np.ascontiguousarray(f32(inputs["L1_ffn_down_scales"]).T)
    g.update(host_constants())
    return g


class Ctx:
    def __init__(self, nc, stack):
        self.nc = nc
        self.stack = stack
        self.sems = {}
        self.cnt = {}

    def sem(self, name):
        if name not in self.sems:
            self.sems[name] = self.stack.enter_context(self.nc.semaphore(name))
            self.cnt[name] = 0
        return self.sems[name]

    def inc(self, inst, name, n=1):
        inst.then_inc(self.sem(name), n)
        self.cnt[name] += n
        return self.cnt[name]

    def wait(self, eng, name, cnt=None):
        eng.wait_ge(self.sem(name), self.cnt[name] if cnt is None else cnt)


def bcast_free(ap, reps, axis_insert=-1):
    new = [list(p) for p in ap.ap] + [[0, reps]]
    return bass.AP(tensor=ap.tensor, offset=ap.offset, ap=new)


STAGEW_WEIGHTS = [
    ("Wx1", 2048, 160),
    ("Wx2A", 128, 2048),
    ("Wx2B", 32, 2048),
    ("Wr", 2048, 1024),
    ("Ww1", 2048, 64),
    ("Ww2", 64, 1024),
    ("Wk", 2048, 1024),
    ("Wv", 2048, 2048),
    ("Wg", 2048, 2048),
    ("Wo", 2048, 2048),
    ("Aff", 2048, 1024),
    ("Bff", 1024, 2048),
    ("Wff", 2048, 2048),
    ("Wqkv", 2048, 6144),
    ("Wout", 2048, 2048),
]

def _stageW_emit(nc, ctx, blk, stack, ins, outs):
    with ExitStack() as sc:
        a = sc.enter_context(nc.sbuf_tensor("wa_w", [128, 8192], F32))
        b = sc.enter_context(nc.sbuf_tensor("wb_w", [128, 8192], BF16))
        b_ld = ctx.cnt["ls"]; b_cv = ctx.cnt["v"]; b_st = ctx.cnt["sg"]
        jobs = []
        for name, r, c in STAGEW_WEIGHTS:
            for r0 in range(0, r, 128):
                rr = min(128, r - r0)
                for c0 in range(0, c, 4096):
                    cc = min(4096, c - c0)
                    jobs.append((name, r0, rr, c0, cc))
        half = 8192 // 2
        for ji, (name, r0, rr, c0, cc) in enumerate(jobs):
            par = ji % 2
            cs = slice(par * half, par * half + cc)
            def loadf(e, name=name, r0=r0, rr=rr, c0=c0, cc=cc, cs=cs, ji=ji):
                if ji >= 2:
                    ctx.wait(e, "v", b_cv + (ji - 2) + 1)
                ctx.inc(e.dma_start(out=a[0:rr, cs], in_=ins[name][r0:r0 + rr, c0:c0 + cc]), "ls", 16)
            blk.sync(loadf)
            def conv(v, name=name, r0=r0, rr=rr, c0=c0, cc=cc, cs=cs, ji=ji):
                ctx.wait(v, "ls", b_ld + 16 * (ji + 1))
                if ji >= 2:
                    ctx.wait(v, "sg", b_st + 16 * (ji - 1))
                v.tensor_copy(b[0:rr, cs], a[0:rr, cs])
                v.drain()
                ctx.inc(v.nop(), "v", 1)
            blk.vector(conv)
            def store(e, name=name, r0=r0, rr=rr, c0=c0, cc=cc, cs=cs, ji=ji):
                ctx.wait(e, "sg")
                ctx.wait(e, "v", b_cv + ji + 1)
                ctx.inc(e.dma_start(out=outs[name][r0:r0 + rr, c0:c0 + cc], in_=b[0:rr, cs]), "sg", 16)
            blk.gpsimd(store)
        blk.gpsimd(lambda e: ctx.wait(e, "sg"))

def build_stageW():
    nc = bass.Bass("TRN2", target_bir_lowering=False)
    ins, outs = {}, {}
    for name, r, c in STAGEW_WEIGHTS:
        ins[name] = nc.dram_tensor(name, [r, c], F32, kind="ExternalInput").ap()
        outs[name] = nc.dram_tensor(name + "_b", [r, c], BF16, kind="ExternalOutput").ap()
    with ExitStack() as stack:
        ctx = Ctx(nc, stack)
        blk = stack.enter_context(nc.Block())
        for s_ in ("ls", "v", "sg"):
            ctx.sem(s_)
        _stageW_emit(nc, ctx, blk, stack, ins, outs)
    return nc

def run_stageW(g, n_cores=8):
    nc = build_stageW()
    in_map = {name: g[name] for name, _, _ in STAGEW_WEIGHTS}
    res = run_bass_kernel_spmd(nc, [in_map] * n_cores, core_ids=list(range(n_cores)))
    return [{name + "_b": r[name + "_b"] for name, _, _ in STAGEW_WEIGHTS} for r in res.results]

from types import SimpleNamespace

STAGEA_SMALL_F32 = [
    ("mu_cols", [128, 16]), ("bw_cols", [128, 8]),
    ("u128", [128, 1024]), ("gnw_t", [128, 2048]), ("gnb_t", [128, 2048]),
    ("bq_cols", [128, 16]), ("bk_cols", [128, 16]),
    ("ones_row_f32", [128, 128]), ("resetmask", [128, 512]),
]
STAGEA_CONST_BF = [
    ("Mpair", [16, 120]), ("Prow", [16, 120]), ("Pcol", [16, 120]),
    ("ProwShift", [8 * 120, 128]), ("Mtail", [16, 16]), ("ident", [128, 128]),
]
STAGEA_ROWS = [("bw_row", 0, [1, 1024]), ("bff_row", 32, [1, 2048]), ("bv_row", 64, [1, 2048])]
STAGEA_SCRATCH = [
    ("rtok_d", [T, 1024], BF16), ("ktok_d", [T, 1024], BF16), ("vtok_d", [T, 2048], BF16),
    ("ewhi_d", [T, 1024], BF16), ("ewlo_d", [T, 1024], BF16), ("gtok_d", [T, 2048], BF16),
    ("o_d", [T, 2048], F32), ("ointer_d", [T, 2048], F32),
]
STAGEA_OUT = [
    ("x1_d", [T, D], F32), ("xn1T_d", [D, T], BF16), ("qT_d", [D, T], BF16),
    ("kT_d", [D, T], BF16), ("v1tok_d", [T, D], BF16),
]
STAGEA_WB = [
    ("Wx1", [2048, 160]), ("Wx2A", [128, 2048]), ("Wx2B", [32, 2048]),
    ("Wr", [2048, 1024]), ("Ww1", [2048, 64]), ("Ww2", [64, 1024]),
    ("Wk", [2048, 1024]), ("Wv", [2048, 2048]), ("Wg", [2048, 2048]),
    ("Wo", [2048, 2048]), ("Aff", [2048, 1024]), ("Bff", [1024, 2048]),
    ("Wff", [2048, 2048]), ("Wqkv", [2048, 6144]),
]

def host_resetmask():
    m = np.ones((128, 512), np.float32)
    m[:, 0::16] = 0.0
    return m

def strided_ap(base_ap, off_elems, dims):
    return bass.AP(tensor=base_ap.tensor, offset=base_ap.offset + off_elems,
                   ap=[list(base_ap.ap[0])] + [list(d) for d in dims])

PSLIST = lambda B: (B.ps0, B.ps1, B.ps2, B.ps3)

def build_stageA(n_chunks=NCH, dumps=(), scratch_out=False, upto='tail'):
    nc = bass.Bass("TRN2", target_bir_lowering=False)
    dram = {}
    dram["x0"] = nc.dram_tensor("x0", [T, D], F32, kind="ExternalInput").ap()
    for name, sh in STAGEA_WB:
        dram[name] = nc.dram_tensor(name + "_b", sh, BF16, kind="ExternalInput").ap()
    for name, sh in STAGEA_SMALL_F32:
        dram[name] = nc.dram_tensor(name, sh, F32, kind="ExternalInput").ap()
    for name, _, sh in STAGEA_ROWS:
        dram[name] = nc.dram_tensor(name, sh, F32, kind="ExternalInput").ap()
    for name, sh in STAGEA_CONST_BF:
        dram[name] = nc.dram_tensor(name, sh, BF16, kind="ExternalInput").ap()
    sk = "ExternalOutput" if scratch_out else "Internal"
    for name, sh, dt in STAGEA_SCRATCH:
        dram[name] = nc.dram_tensor(name, sh, dt, kind=sk).ap()
    for name, sh, dt in STAGEA_OUT:
        dram[name] = nc.dram_tensor(name, sh, dt, kind="ExternalOutput").ap()
    for name in dumps:
        dt = BF16 if name in ("xn", "og") else F32
        dram["dump_" + name] = nc.dram_tensor("dump_" + name, [T, 2048], dt,
                                              kind="ExternalOutput").ap()

    with ExitStack() as stack:
        ctx = Ctx(nc, stack)
        blk = stack.enter_context(nc.Block())
        _stageA_emit(nc, ctx, blk, stack, dram, n_chunks, dumps, upto)
    return nc

def _stageA_emit(nc, ctx, blk, stack0, dram, n_chunks=NCH, dumps=(), upto='tail'):
    with ExitStack() as stack:
        sb = lambda name, shape, dt=F32: stack.enter_context(nc.sbuf_tensor(name, shape, dt))

        P = SimpleNamespace()
        P.wx1 = sb("wx1", [128, 16 * 160], BF16)
        P.wx2A = sb("wx2A", [128, 2048], BF16)
        P.wx2B = sb("wx2B", [32, 2048], BF16)
        P.ww1 = sb("ww1", [128, 16 * 64], BF16)
        P.ww2 = sb("ww2", [64, 1024], BF16)
        P.mu_c = sb("mu_c", [128, 16]); P.bw_c = sb("bw_c", [128, 8])
        P.bq_c = sb("bq_c", [128, 16]); P.bk_c = sb("bk_c", [128, 16])
        P.rows = sb("rows", [128, 2048])
        P.ones_rf = sb("ones_rf", [128, 128])
        P.rmask = sb("rmask", [128, 512])
        P.mpair = sb("mpair", [16, 120], BF16); P.prow = sb("prow", [16, 120], BF16)
        P.pcol = sb("pcol", [16, 120], BF16)
        P.prsh = sb("prsh", [120, 8 * 128], BF16)
        P.mtail = sb("mtail", [16, 16], BF16)
        P.ident = sb("identc", [128, 128], BF16)
        P.xnT = sb("xnT", [128, 16 * CH], BF16)
        P.xn_carry = sb("xn_carry", [128, 16], BF16)
        P.Scar = sb("Scar", [128, 8 * 64])
        P.dbuf = sb("dbuf", [128, 4 * 32])
        P.Dcol = sb("Dcol", [128, 8 * 32]); P.D0 = sb("D0", [128, 8 * 32])
        P.rr = sb("rrb", [128, 8 * CH], BF16)
        P.tA = sb("tA", [128, 2048]); P.tB = sb("tB", [128, 2048])
        P.bfA = sb("bfA", [128, 2048], BF16); P.bfB = sb("bfB", [128, 2048], BF16)
        P.colA = sb("colA", [128, 8]); P.colB = sb("colB", [128, 8])
        P.gcol = sb("gcol", [128, 4 * 32])
        P.psA4 = stack.enter_context(nc.psum_tensor("psA4", [128, 2048], F32))
        P.ps0 = P.psA4[:, 0:512]
        P.ps1 = P.psA4[:, 512:1024]
        P.ps2 = P.psA4[:, 1024:1536]
        P.ps3 = P.psA4[:, 1536:2048]

        for s_ in ("ls", "lc", "sg", "t", "v", "s", "w0", "w1", "q0", "q1"):
            ctx.sem(s_)

        def load_small(e):
            pairs = [
                (P.wx1[:].rearrange("p (kt m) -> p kt m", kt=16), dram["Wx1"].rearrange("(kt p) m -> p kt m", p=128)),
                (P.wx2A[:], dram["Wx2A"]), (P.wx2B[:], dram["Wx2B"]),
                (P.ww1[:].rearrange("p (kt m) -> p kt m", kt=16), dram["Ww1"].rearrange("(kt p) m -> p kt m", p=128)),
                (P.ww2[:], dram["Ww2"]),
                (P.mu_c[:], dram["mu_cols"]), (P.bw_c[:], dram["bw_cols"]),
                (P.bq_c[:], dram["bq_cols"]), (P.bk_c[:], dram["bk_cols"]),
                (P.ones_rf[:], dram["ones_row_f32"]), (P.rmask[:], dram["resetmask"]),
                (P.mpair[:], dram["Mpair"]), (P.prow[:], dram["Prow"]), (P.pcol[:], dram["Pcol"]),
                (P.prsh[:].rearrange("p (j m) -> p j m", j=8), dram["ProwShift"].rearrange("(j p) m -> p j m", p=120)),
                (P.mtail[:], dram["Mtail"]), (P.ident[:], dram["ident"]),
            ]
            for name, row, sh in STAGEA_ROWS:
                pairs.append((P.rows[row:row + 1, 0:sh[1]], dram[name]))
            for o, i in pairs:
                ctx.inc(e.dma_start(out=o, in_=i, allow_slow_non_contiguous=True), "ls", 16)
        blk.sync(load_small)
        def init_bufs(v):
            ctx.wait(v, "ls")
            v.memset(P.xn_carry[:], 0.0)
            v.memset(P.Scar[:], 0.0)
            v.drain()
            ctx.inc(v.nop(), "v", 1)
        blk.vector(init_bufs)

        st8 = SimpleNamespace(slab_free=[0, 0], lslab_free=[0, 0], ccfree=[0, 0])
        for ci in range(n_chunks):
            _emit_chunk(nc, ctx, blk, stack, dram, P, st8, ci, ci * CH, dumps, upto)

        def fin(e):
            ctx.wait(e, "sg"); ctx.wait(e, "ls"); ctx.wait(e, "lc")
        blk.gpsimd(fin)

def _phase_barrier(ctx, blk, engines=("tensor", "vector", "scalar", "sync", "gpsimd")):
    for en in engines:
        def w(e):
            for s_ in ("t", "v", "s", "sg"):
                ctx.wait(e, s_)
        getattr(blk, en)(w)

def _load_halfslab(ctx, blk, dram, slab, wname, nc512, kh, free_cnt, width=512, kts=8, sem="w0"):
    def ld(e, wname=wname, nc512=nc512, kh=kh, free_cnt=free_cnt, kts=kts, sem=sem):
        ctx.wait(e, sem)
        if free_cnt > 0:
            ctx.wait(e, "t", free_cnt)
        src = dram[wname].rearrange("(kt p) n -> p kt n", p=128)[:, kts * kh:kts * (kh + 1),
                                                                width * nc512:width * (nc512 + 1)]
        ctx.inc(e.dma_start(out=slab[:, 0:kts * width].rearrange("p (kt n) -> p kt n", kt=kts),
                            in_=src, allow_slow_non_contiguous=True), sem, 16)
    blk.sync(ld)
    return ctx.cnt[sem]

def _emit_chunk(nc, ctx, blk, stack, dram, P, st8, ci, t0, dumps, upto='tail'):
    with ExitStack() as sc:
        sb = lambda name, shape, dt=F32: sc.enter_context(nc.sbuf_tensor(name + f"_{ci}", shape, dt))
        S = SimpleNamespace()
        S.deltaT = sb("deltaT", [128, 16 * CH], BF16)
        S.uT = sb("uT", [128, 16 * CH], BF16)
        S.zA = sb("zA", [128, CH], BF16); S.zB = sb("zB", [32, CH], BF16)
        S.h1 = sb("h1", [64, CH], BF16)
        S.rT = sb("rT", [128, 8 * CH], BF16)
        S.rtok = sb("rtok", [128, 4 * 1024], BF16)
        S.ktok = sb("ktok", [128, 4 * 1024], BF16)
        S.ewT = sb("ewT", [128, 8 * CH])
        S.scanS = sb("scanS", [128, 8 * CH])
        S.ewtk = sb("ewtk", [128, 1024])
        S.vp = sb("vp", [128, 4 * 512], BF16)
        S.u128 = sb("u128s", [128, 1024])
        S.wslab = [sb(f"wslab{i}", [128, 8 * 512], BF16) for i in range(2)]
        S.pst = sc.enter_context(nc.psum_tensor(f"pst_{ci}", [128, 1024], BF16))
        _phase_barrier(ctx, blk, ("sync", "vector", "scalar", "tensor"))
        scope = nc.named_scope
        def _ldu(e):
            ctx.wait(e, "ls")
            ctx.inc(e.dma_start(out=S.u128[:], in_=dram["u128"]), "ls", 16)
        blk.sync(_ldu)
        with scope(f"p1_{ci}"):
            _emit_p1(nc, ctx, blk, dram, P, S, ci, t0, dumps)
        if upto != 'p1':
            with scope(f"p2_{ci}"):
                _emit_branches(nc, ctx, blk, dram, P, S, st8, ci, t0, dumps)
        _phase_barrier(ctx, blk)
    if upto in ('p1', 'p2'):
        return
    with ExitStack() as sc:
        sb = lambda name, shape, dt=F32: sc.enter_context(nc.sbuf_tensor(name + f"_{ci}", shape, dt))
        R = SimpleNamespace()
        R.dS = sb("dS", [128, 8 * 64 * 32])
        R.ccr = [sb(f"ccr{i}", [16, 1024], BF16) for i in range(2)]
        R.cck = [sb(f"cck{i}", [16, 1024], BF16) for i in range(2)]
        R.ccv = [sb(f"ccv{i}", [16, 2048], BF16) for i in range(2)]
        R.cwh = [sb(f"cwh{i}", [16, 1024], BF16) for i in range(2)]
        R.cwl = [sb(f"cwl{i}", [16, 1024], BF16) for i in range(2)]
        R.ppair = sb("ppair", [120, 1024], BF16)
        R.t1b = sb("t1b", [120, 1024], BF16)
        R.t2b = sb("t2b", [120, 1024], BF16)
        R.Ab = sb("Ab", [120, 32])
        R.t3b = sb("t3b", [120, 2048], BF16)
        R.khat = sb("khat", [16, 1024], BF16)
        R.texp = sb("texp", [16, 1024], BF16)
        R.oint = sb("oint", [16, 2048])
        R.dsbf = [sb(f"dsbf{i}", [128, 512], BF16) for i in range(2)]
        R.oacc = sb("oacc", [128, 2048])
        R.psB4 = sc.enter_context(nc.psum_tensor(f"psB4_{ci}", [128, 2048], F32))
        _phase_barrier(ctx, blk)
        with nc.named_scope(f"p3_{ci}"):
            _emit_rwkv(nc, ctx, blk, dram, P, R, st8, ci, t0, dumps, upto)
        _phase_barrier(ctx, blk)
    if upto.startswith('rwkv'):
        return
    with ExitStack() as sc:
        sb = lambda name, shape, dt=F32: sc.enter_context(nc.sbuf_tensor(name + f"_{ci}", shape, dt))
        Q = SimpleNamespace()
        Q.wslab = [sb(f"twslab{i}", [128, 8 * 512], BF16) for i in range(2)]
        Q.lslab = [sb(f"tlslab{i}", [128, 16 * 128], BF16) for i in range(2)]
        Q.gnw = sb("gnw", [128, 2048]); Q.gnb = sb("gnb", [128, 2048])
        Q.u128 = sb("u128", [128, 1024])
        Q.vecsb = sb("vecsb", [128, 4 * 1024], BF16)
        Q.arro = sb("arro", [128, 256])
        Q.rout = sb("rout", [128, 4 * 64])
        Q.vsm = sb("vsm", [128, 4 * 16])
        Q.u2b = sb("u2b", [128, 4 * 1024], BF16)
        Q.u2T = sb("u2T", [128, 4 * 1024], BF16)
        Q.ogT = sb("ogT", [128, 4 * 2048], BF16)
        Q.xn1T = sb("xn1T", [128, 16 * CH], BF16)
        Q.obuf = sb("obuf", [128, 2048])
        Q.pst = sc.enter_context(nc.psum_tensor(f"pst2_{ci}", [128, 1024], BF16))
        _phase_barrier(ctx, blk)
        with nc.named_scope(f"p4_{ci}"):
            _emit_tail(nc, ctx, blk, dram, P, Q, st8, ci, t0, dumps)
        _phase_barrier(ctx, blk)

def _emit_p1(nc, ctx, blk, dram, P, S, ci, t0, dumps):
    for tt in range(4):
        def ldx(e, tt=tt):
            ctx.wait(e, "ls")
            ctx.wait(e, "v")
            ctx.inc(e.dma_start(out=P.tA[:], in_=dram["x0"][t0 + 128 * tt: t0 + 128 * (tt + 1), :]), "ls", 16)
        blk.sync(ldx)
        def p1v(v, tt=tt):
            ctx.wait(v, "ls")
            v.tensor_reduce(P.colA[:, 0:1], P.tA[:], AX, OP.add)
            v.drain(); ctx.inc(v.nop(), "v", 1)
        blk.vector(p1v)
        def p1s(s, tt=tt):
            ctx.wait(s, "ls")
            s.activation(P.tB[:], P.tA[:], AF.Square, accum_out=P.colB[:, 0:1])
            s.drain(); ctx.inc(s.nop(), "s", 1)
        blk.scalar(p1s)
        def p1v2(v, tt=tt):
            ctx.wait(v, "s")
            v.tensor_scalar(P.colA[:, 1:2], P.colA[:, 0:1], 1.0 / D, None, OP.mult)
            v.tensor_scalar(P.colB[:, 1:2], P.colB[:, 0:1], 1.0 / D, None, OP.mult)
            v.drain()
            v.tensor_scalar(P.colA[:, 2:3], P.colA[:, 1:2], P.colA[:, 1:2], -1.0, OP.mult, OP.mult)
            v.drain()
            v.tensor_tensor(P.colB[:, 2:3], P.colB[:, 1:2], P.colA[:, 2:3], OP.add)
            v.drain()
            v.tensor_scalar(P.colB[:, 4:5], P.colB[:, 2:3], LN_EPS, None, OP.add)
            v.drain(); ctx.inc(v.nop(), "v", 1)
        blk.vector(p1v2)
        def p1s2(s, tt=tt):
            ctx.wait(s, "v")
            s.activation(P.colB[:, 3:4], P.colB[:, 4:5], AF.Sqrt)
            s.drain(); ctx.inc(s.nop(), "s", 1)
        blk.scalar(p1s2)
        def p1v3(v, tt=tt):
            ctx.wait(v, "s")
            ctx.wait(v, "sg")
            v.reciprocal(P.colA[:, 3:4], P.colB[:, 3:4])
            v.drain()
            v.tensor_scalar(P.colA[:, 4:5], P.colA[:, 1:2], P.colA[:, 3:4], -1.0, OP.mult, OP.mult)
            v.drain()
            v.tensor_scalar(P.bfA[:], P.tA[:], P.colA[:, 3:4], P.colA[:, 4:5], OP.mult, OP.add)
            v.drain(); ctx.inc(v.nop(), "v", 1)
        blk.vector(p1v3)
        if "xn" in dumps:
            def dxn(e, tt=tt):
                ctx.wait(e, "v")
                ctx.inc(e.dma_start(out=dram["dump_xn"][t0 + 128 * tt: t0 + 128 * (tt + 1), :],
                                    in_=P.bfA[:]), "sg", 16)
            blk.gpsimd(dxn)
        for fh in range(2):
            def p1t(t, tt=tt, fh=fh):
                ctx.wait(t, "v")
                for j in range(8):
                    ft = 8 * fh + j
                    _li = t.transpose(S.pst[:, 128 * j:128 * (j + 1)], P.bfA[:, 128 * ft:128 * (ft + 1)], P.ident[:])
                ctx.inc(_li, "t", 1)
            blk.tensor(p1t)
            def p1e(v, tt=tt, fh=fh):
                ctx.wait(v, "t")
                dst = strided_ap(P.xnT[:], CH * 8 * fh + 128 * tt, [[CH, 8], [1, 128]])
                v.tensor_copy(dst, S.pst[:].rearrange("p (j m) -> p j m", j=8))
                v.drain(); ctx.inc(v.nop(), "v", 1)
            blk.vector(p1e)
    def p1d(v):
        for ft in range(16):
            o = CH * ft
            v.tensor_tensor(S.deltaT[:, o + 1:o + CH], P.xnT[:, o:o + CH - 1], P.xnT[:, o + 1:o + CH], OP.subtract)
            v.tensor_tensor(S.deltaT[:, o:o + 1], P.xn_carry[:, ft:ft + 1], P.xnT[:, o:o + 1], OP.subtract)
        v.drain()
        for ft in range(16):
            v.tensor_copy(P.xn_carry[:, ft:ft + 1], P.xnT[:, CH * ft + CH - 1:CH * ft + CH])
        v.drain()
        for ft in range(16):
            o = CH * ft
            v.scalar_tensor_tensor(S.uT[:, o:o + CH], S.deltaT[:, o:o + CH], P.mu_c[:, ft:ft + 1],
                                   P.xnT[:, o:o + CH], OP.mult, OP.add)
        v.drain(); ctx.inc(v.nop(), "v", 1)
    blk.vector(p1d)
    def zmm(t):
        ctx.wait(t, "v")
        for kt in range(16):
            _li = t.matmul(P.ps0[:], P.wx1[:, 160 * kt:160 * kt + 128], S.uT[:, CH * kt:CH * (kt + 1)],
                     start=(kt == 0), stop=(kt == 15))
        for kt in range(16):
            _li = t.matmul(P.ps1[0:32, :], P.wx1[:, 160 * kt + 128:160 * (kt + 1)], S.uT[:, CH * kt:CH * (kt + 1)],
                     start=(kt == 0), stop=(kt == 15))
        ctx.inc(_li, "t", 1)
    blk.tensor(zmm)
    def zev(s):
        ctx.wait(s, "t")
        s.activation(S.zA[:], P.ps0[:], AF.Tanh)
        s.activation(S.zB[:], P.ps1[0:32, :], AF.Tanh)
        s.drain(); ctx.inc(s.nop(), "s", 1)
    blk.scalar(zev)

def _emit_ubuild(ctx, blk, P, S, b):
    vdone = []
    def emit_mm(fp):
        def mm(t, b=b, fp=fp):
            if fp == 0:
                ctx.wait(t, "s")
            if fp >= 2:
                ctx.wait(t, "v", vdone[fp - 2])
            for j in range(2):
                ft = 2 * fp + j
                psd = (P.ps0 if j == 0 else P.ps1) if fp % 2 == 0 else (P.ps2 if j == 0 else P.ps3)
                if b < 4:
                    _li = t.matmul(psd[:], P.wx2A[32 * b:32 * (b + 1), 128 * ft:128 * (ft + 1)],
                             S.zA[32 * b:32 * (b + 1), :], start=True, stop=True,
                             tile_position=(32 * b, 0))
                else:
                    _li = t.matmul(psd[:], P.wx2B[:, 128 * ft:128 * (ft + 1)], S.zB[:], start=True, stop=True,
                             tile_position=(0, 0))
            ctx.inc(_li, "t", 1)
        blk.tensor(mm)
        return ctx.cnt["t"]
    tdone = []
    def emit_vv(fp):
        def vv(v, b=b, fp=fp):
            ctx.wait(v, "t", tdone[fp])
            ctx.wait(v, "sg")
            for j in range(2):
                ft = 2 * fp + j
                o = CH * ft
                psd = (P.ps0 if j == 0 else P.ps1) if fp % 2 == 0 else (P.ps2 if j == 0 else P.ps3)
                v.tensor_tensor(P.bfB[:, 512 * j:512 * (j + 1)], psd[:], S.deltaT[:, o:o + CH], OP.mult)
            v.drain()
            for j in range(2):
                ft = 2 * fp + j
                o = CH * ft
                v.tensor_tensor(S.uT[:, o:o + CH], P.bfB[:, 512 * j:512 * (j + 1)], P.xnT[:, o:o + CH], OP.add)
            v.drain(); ctx.inc(v.nop(), "v", 1)
        blk.vector(vv)
        vdone.append(ctx.cnt["v"])
    for fp in range(8):
        tdone.append(emit_mm(fp))
        if fp >= 1:
            emit_vv(fp - 1)
    emit_vv(7)

def _tok_pass(nc, ctx, blk, dram, P, S, st8, t0, wname, ncount, evac_fn, store_fn):
    for nck in range(ncount):
        for kh in range(2):
            par = (nck * 2 + kh) % 2
            lcnt = _load_halfslab(ctx, blk, dram, S.wslab[par], wname, nck, kh, st8.slab_free[par], sem=f"w{par}")
            def mm(t, nck=nck, kh=kh, par=par, lcnt=lcnt):
                ctx.wait(t, "v")
                ctx.wait(t, f"w{par}", lcnt)
                if kh == 0:
                    ctx.wait(t, "s")
                for tt in range(4):
                    psd = PSLIST(P)[tt]
                    for j in range(8):
                        kt = 8 * kh + j
                        _li = t.matmul(psd[:], S.uT[:, CH * kt + 128 * tt:CH * kt + 128 * (tt + 1)],
                                 S.wslab[par][:, 512 * j:512 * (j + 1)],
                                 start=(kh == 0 and j == 0), stop=(kh == 1 and j == 7))
                ctx.inc(_li, "t", 1)
            blk.tensor(mm)
            st8.slab_free[par] = ctx.cnt["t"]
        def ev(s, nck=nck):
            ctx.wait(s, "t")
            if store_fn is not None:
                ctx.wait(s, "sg")
            for tt in range(4):
                evac_fn(s, tt, nck, PSLIST(P)[tt])
            s.drain(); ctx.inc(s.nop(), "s", 1)
        blk.scalar(ev)
        if store_fn is not None:
            def stf(e, nck=nck):
                ctx.wait(e, "sg")
                ctx.wait(e, "s")
                store_fn(e, nck)
            blk.gpsimd(stf)

def _emit_branches(nc, ctx, blk, dram, P, S, st8, ci, t0, dumps):
    _emit_ubuild(ctx, blk, P, S, 0)
    for nc2 in range(2):
        for kh in range(2):
            par = (nc2 * 2 + kh) % 2
            lcnt = _load_halfslab(ctx, blk, dram, S.wslab[par], "Wr", nc2, kh, st8.slab_free[par], sem=f"w{par}")
            def rmm(t, nc2=nc2, kh=kh, par=par, lcnt=lcnt):
                ctx.wait(t, "v")
                ctx.wait(t, f"w{par}", lcnt)
                if kh == 0:
                    ctx.wait(t, "s")
                for j in range(4):
                    psd = PSLIST(P)[j]
                    for jj in range(8):
                        _li = t.matmul(psd[:], S.wslab[par][:, 512 * jj + 128 * j:512 * jj + 128 * (j + 1)],
                                 S.uT[:, CH * (8 * kh + jj):CH * (8 * kh + jj + 1)],
                                 start=(kh == 0 and jj == 0), stop=(kh == 1 and jj == 7))
                ctx.inc(_li, "t", 1)
            blk.tensor(rmm)
            st8.slab_free[par] = ctx.cnt["t"]
        def rev(s, nc2=nc2):
            ctx.wait(s, "t")
            for j in range(4):
                mt = 4 * nc2 + j
                s.activation(S.rT[:, CH * mt:CH * (mt + 1)], PSLIST(P)[j][:], AF.Copy)
            s.drain(); ctx.inc(s.nop(), "s", 1)
        blk.scalar(rev)
    def r_evac(s, tt, nck, psd):
        s.activation(S.rtok[:, 1024 * tt + 512 * nck:1024 * tt + 512 * (nck + 1)], psd[:], AF.Copy)
    _tok_pass(nc, ctx, blk, dram, P, S, st8, t0, "Wr", 2, r_evac, None)
    def rst(e):
        ctx.wait(e, "sg")
        ctx.wait(e, "s")
        for tt in range(4):
            ctx.inc(e.dma_start(out=dram["rtok_d"][t0 + 128 * tt:t0 + 128 * (tt + 1), :],
                                in_=S.rtok[:, 1024 * tt:1024 * (tt + 1)]), "sg", 16)
    blk.gpsimd(rst)

    _emit_ubuild(ctx, blk, P, S, 1)
    def wmm(t):
        ctx.wait(t, "v")
        for kt in range(16):
            _li = t.matmul(P.ps0[0:64, :], P.ww1[:, 64 * kt:64 * (kt + 1)], S.uT[:, CH * kt:CH * (kt + 1)],
                     start=(kt == 0), stop=(kt == 15))
        ctx.inc(_li, "t", 1)
    blk.tensor(wmm)
    def wev(s):
        ctx.wait(s, "t")
        s.activation(S.h1[:], P.ps0[0:64, :], AF.Tanh)
        s.drain(); ctx.inc(s.nop(), "s", 1)
    blk.scalar(wev)
    for mp in range(4):
        def ewm(t, mp=mp):
            ctx.wait(t, "s")
            for j in range(2):
                mt = 2 * mp + j
                psd = P.ps0 if j == 0 else P.ps1
                _li = t.matmul(psd[:], P.ww2[:, 128 * mt:128 * (mt + 1)], S.h1[:], start=True, stop=True)
            ctx.inc(_li, "t", 1)
        blk.tensor(ewm)
        def ewe(s, mp=mp):
            ctx.wait(s, "t")
            for j in range(2):
                mt = 2 * mp + j
                psd = P.ps0 if j == 0 else P.ps1
                s.activation(S.ewT[:, CH * mt:CH * (mt + 1)], psd[:], AF.Exp, bias=P.bw_c[:, mt:mt + 1])
            s.drain(); ctx.inc(s.nop(), "s", 1)
        blk.scalar(ewe)
    def wsc(v):
        ctx.wait(v, "s")
        for mt in range(8):
            o = CH * mt
            v.tensor_tensor_scan(S.scanS[:, o:o + CH], P.rmask[:], S.ewT[:, o:o + CH], 0.0, OP.mult, OP.add)
        v.drain(); ctx.inc(v.nop(), "v", 1)
    blk.vector(wsc)
    for half in range(2):
        def wra(v, half=half):
            ctx.wait(v, "v" if half else "v")
            for j in range(4):
                mt = 4 * half + j
                o = CH * mt
                v.tensor_tensor(P.tA[:, CH * j:CH * (j + 1)], S.ewT[:, o:o + CH], S.scanS[:, o:o + CH], OP.subtract)
            v.drain(); ctx.inc(v.nop(), "v", 1)
        blk.vector(wra)
        def wrs(s, half=half):
            ctx.wait(s, "v")
            s.activation(P.tB[:], P.tA[:], AF.Exp)
            s.drain(); ctx.inc(s.nop(), "s", 1)
        blk.scalar(wrs)
        def wrm(v, half=half):
            ctx.wait(v, "s")
            for j in range(4):
                mt = 4 * half + j
                o = CH * mt
                v.tensor_tensor(P.rr[:, o:o + CH], S.rT[:, o:o + CH], P.tB[:, CH * j:CH * (j + 1)], OP.mult)
            v.drain(); ctx.inc(v.nop(), "v", 1)
        blk.vector(wrm)
    def wdc(s):
        ctx.wait(s, "v")
        for mt in range(8):
            src = strided_ap(S.scanS[:], CH * mt + 15, [[16, 32]])
            s.activation(P.Dcol[:, 32 * mt:32 * (mt + 1)], src, AF.Exp, scale=-1.0)
        s.drain(); ctx.inc(s.nop(), "s", 1)
    blk.scalar(wdc)
    def wd0(v):
        ctx.wait(v, "s")
        v.tensor_copy(P.D0[:], P.Dcol[:])
        v.drain()
        for mt in range(8):
            v.memset(P.D0[:, 32 * mt:32 * mt + 1], 0.0)
        v.drain(); ctx.inc(v.nop(), "v", 1)
    blk.vector(wd0)
    for tt in range(4):
        def ewt(t, tt=tt):
            ctx.wait(t, "s" if tt == 0 else "v")
            for j in range(2):
                psd = P.ps2 if j == 0 else P.ps3
                _li = t.matmul(psd[:], S.h1[:, 128 * tt:128 * (tt + 1)], P.ww2[:, 512 * j:512 * (j + 1)],
                         start=True, stop=False)
                _li = t.matmul(psd[:], P.ones_rf[0:1, :], P.rows[0:1, 512 * j:512 * (j + 1)],
                         start=False, stop=True)
            ctx.inc(_li, "t", 1)
        blk.tensor(ewt)
        def ewte(s, tt=tt):
            ctx.wait(s, "t")
            s.activation(S.ewtk[:, 0:512], P.ps2[:], AF.Exp)
            s.activation(S.ewtk[:, 512:1024], P.ps3[:], AF.Exp)
            s.drain(); ctx.inc(s.nop(), "s", 1)
        blk.scalar(ewte)
        def ewtv(v, tt=tt):
            ctx.wait(v, "s")
            ctx.wait(v, "sg")
            v.tensor_copy(P.bfA[:, 0:1024], S.ewtk[:])
            v.drain()
            v.tensor_tensor(P.bfB[:, 0:1024], S.ewtk[:], P.bfA[:, 0:1024], OP.subtract)
            v.drain(); ctx.inc(v.nop(), "v", 1)
        blk.vector(ewtv)
        def ewts(e, tt=tt):
            ctx.wait(e, "sg")
            ctx.wait(e, "v")
            ctx.inc(e.dma_start(out=dram["ewhi_d"][t0 + 128 * tt:t0 + 128 * (tt + 1), :], in_=P.bfA[:, 0:1024]), "sg", 16)
            ctx.inc(e.dma_start(out=dram["ewlo_d"][t0 + 128 * tt:t0 + 128 * (tt + 1), :], in_=P.bfB[:, 0:1024]), "sg", 16)
        blk.gpsimd(ewts)

    _emit_ubuild(ctx, blk, P, S, 2)
    def k_evac(s, tt, nck, psd):
        s.activation(S.ktok[:, 1024 * tt + 512 * nck:1024 * tt + 512 * (nck + 1)], psd[:], AF.Copy)
    _tok_pass(nc, ctx, blk, dram, P, S, st8, t0, "Wk", 2, k_evac, None)
    def kst(e):
        ctx.wait(e, "sg")
        ctx.wait(e, "s")
        for tt in range(4):
            ctx.inc(e.dma_start(out=dram["ktok_d"][t0 + 128 * tt:t0 + 128 * (tt + 1), :],
                                in_=S.ktok[:, 1024 * tt:1024 * (tt + 1)]), "sg", 16)
    blk.gpsimd(kst)
    def dg(v):
        ctx.wait(v, "s")
        ctx.wait(v, "ls")
        for tt in range(4):
            v.tensor_tensor(P.tA[:, 0:1024], S.rtok[:, 1024 * tt:1024 * (tt + 1)],
                            S.ktok[:, 1024 * tt:1024 * (tt + 1)], OP.mult)
            v.drain()
            v.tensor_tensor(P.tB[:, 0:1024], P.tA[:, 0:1024], S.u128[:], OP.mult)
            v.drain()
            v.tensor_reduce(P.dbuf[:, 32 * tt:32 * (tt + 1), None],
                            P.tB[:, 0:1024].rearrange("p (h k) -> p h k", h=32), AX, OP.add)
            v.drain()
        v.drain(); ctx.inc(v.nop(), "v", 1)
    blk.vector(dg)

    _emit_ubuild(ctx, blk, P, S, 3)
    def v_evac(s, tt, nck, psd):
        s.activation(S.vp[:, 512 * tt:512 * (tt + 1)], psd[:], AF.Copy)
    def v_store(e, nck):
        for tt in range(4):
            ctx.inc(e.dma_start(out=dram["vtok_d"][t0 + 128 * tt:t0 + 128 * (tt + 1),
                                                   512 * nck:512 * (nck + 1)],
                                in_=S.vp[:, 512 * tt:512 * (tt + 1)]), "sg", 16)
    _tok_pass(nc, ctx, blk, dram, P, S, st8, t0, "Wv", 4, v_evac, v_store)

    _emit_ubuild(ctx, blk, P, S, 4)
    def g_store(e, nck):
        for tt in range(4):
            ctx.inc(e.dma_start(out=dram["gtok_d"][t0 + 128 * tt:t0 + 128 * (tt + 1),
                                                   512 * nck:512 * (nck + 1)],
                                in_=S.vp[:, 512 * tt:512 * (tt + 1)]), "sg", 16)
    _tok_pass(nc, ctx, blk, dram, P, S, st8, t0, "Wg", 4, v_evac, g_store)

def _emit_rwkv(nc, ctx, blk, dram, P, R, st8, ci, t0, dumps, upto='tail'):
    setA = [P.ps0, P.ps1, P.ps2, P.ps3]
    setB = [R.psB4[:, 512 * n:512 * (n + 1)] for n in range(4)]
    st8.ccfree = [0, 0]
    vt12 = 0
    va3 = [0, 0]
    t_dsm_prev = 0
    for rc in range(NRC):
        tr = t0 + RC * rc
        par = rc % 2
        gi = rc % 8
        ps = setA if par == 0 else setB
        def ccl(e, rc=rc, tr=tr, par=par):
            ctx.wait(e, "lc")
            if st8.ccfree[par] > 0:
                ctx.wait(e, "t", st8.ccfree[par])
            for buf, dn in ((R.ccr, "rtok_d"), (R.cck, "ktok_d"), (R.ccv, "vtok_d"),
                            (R.cwh, "ewhi_d"), (R.cwl, "ewlo_d")):
                ctx.inc(e.dma_start(out=buf[par][:], in_=dram[dn][tr:tr + RC, :]), "lc", 16)
        blk.scalar(ccl)
        lc_cnt = ctx.cnt["lc"]
        def emm(t, rc=rc, par=par, ps=ps, lc_cnt=lc_cnt, w_v=va3[par]):
            ctx.wait(t, "lc", lc_cnt)
            ctx.wait(t, "v", w_v)
            for half, psd in ((0, ps[0]), (1, ps[1])):
                _li = t.matmul(psd[0:120, :], P.mpair[:], R.cwh[par][:, 512 * half:512 * (half + 1)],
                               start=True, stop=False)
                _li = t.matmul(psd[0:120, :], P.mpair[:], R.cwl[par][:, 512 * half:512 * (half + 1)],
                               start=False, stop=True)
            for half, psd in ((0, ps[2]), (1, ps[3])):
                _li = t.matmul(psd[0:16, :], P.mtail[:], R.cwh[par][:, 512 * half:512 * (half + 1)],
                               start=True, stop=False)
                _li = t.matmul(psd[0:16, :], P.mtail[:], R.cwl[par][:, 512 * half:512 * (half + 1)],
                               start=False, stop=True)
            ctx.inc(_li, "t", 1)
        blk.tensor(emm)
        t_emm = ctx.cnt["t"]
        def eex(s, rc=rc, ps=ps, t_emm=t_emm, w_v=vt12):
            ctx.wait(s, "t", t_emm)
            ctx.wait(s, "v", w_v)
            s.activation(R.ppair[:, 0:512], ps[0][0:120, :], AF.Exp)
            s.activation(R.ppair[:, 512:1024], ps[1][0:120, :], AF.Exp)
            s.activation(R.texp[:, 0:512], ps[2][0:16, :], AF.Exp)
            s.activation(R.texp[:, 512:1024], ps[3][0:16, :], AF.Exp)
            s.drain(); ctx.inc(s.nop(), "s", 1)
        blk.scalar(eex)
        s_eex = ctx.cnt["s"]
        def kh_(v, rc=rc, par=par, s_eex=s_eex, w_t=t_dsm_prev):
            ctx.wait(v, "s", s_eex)
            ctx.wait(v, "t", w_t)
            v.tensor_tensor(R.khat[:], R.cck[par][:], R.texp[:], OP.mult)
            v.drain(); ctx.inc(v.nop(), "v", 1)
        blk.vector(kh_)
        def rkm(t, rc=rc, par=par, ps=ps, s_eex=s_eex):
            ctx.wait(t, "s", s_eex)
            for half, psd in ((0, ps[0]), (1, ps[1])):
                _li = t.matmul(psd[0:120, :], P.prow[:], R.ccr[par][:, 512 * half:512 * (half + 1)],
                               start=True, stop=True)
            for half, psd in ((0, ps[2]), (1, ps[3])):
                _li = t.matmul(psd[0:120, :], P.pcol[:], R.cck[par][:, 512 * half:512 * (half + 1)],
                               start=True, stop=True)
            ctx.inc(_li, "t", 1)
        blk.tensor(rkm)
        t_rkm = ctx.cnt["t"]
        def t12(v, rc=rc, ps=ps, t_rkm=t_rkm):
            ctx.wait(v, "t", t_rkm)
            v.tensor_tensor(R.t1b[:, 0:512], R.ppair[:, 0:512], ps[0][0:120, :], OP.mult)
            v.tensor_tensor(R.t1b[:, 512:1024], R.ppair[:, 512:1024], ps[1][0:120, :], OP.mult)
            v.drain()
            v.tensor_tensor(R.t2b[:, 0:512], R.t1b[:, 0:512], ps[2][0:120, :], OP.mult)
            v.tensor_tensor(R.t2b[:, 512:1024], R.t1b[:, 512:1024], ps[3][0:120, :], OP.mult)
            v.drain()
            v.tensor_reduce(R.Ab[:, :, None], R.t2b[:].rearrange("p (h k) -> p h k", h=32), AX, OP.add)
            v.drain(); ctx.inc(v.nop(), "v", 1)
        blk.vector(t12)
        vt12 = ctx.cnt["v"]
        t3half_v = []
        for vh_ in range(2):
            def vrm(t, rc=rc, par=par, ps=ps, vh_=vh_, w_v=vt12):
                ctx.wait(t, "v", w_v)
                for q in (2 * vh_, 2 * vh_ + 1):
                    _li = t.matmul(ps[q][0:120, :], P.pcol[:], R.ccv[par][:, 512 * q:512 * (q + 1)],
                                   start=True, stop=True)
                ctx.inc(_li, "t", 1)
            blk.tensor(vrm)
            t_vrm = ctx.cnt["t"]
            def t3c(v, rc=rc, ps=ps, vh_=vh_, t_vrm=t_vrm):
                ctx.wait(v, "t", t_vrm)
                for q in (2 * vh_, 2 * vh_ + 1):
                    src = ps[q][0:120, :].rearrange("p (h v) -> p h v", h=8)
                    av = strided_ap(R.Ab[0:120, :], 8 * q, [[1, 8], [0, 64]])
                    dst = R.t3b[:, 512 * q:512 * (q + 1)].rearrange("p (h v) -> p h v", h=8)
                    v.tensor_tensor(dst, src, av, OP.mult)
                v.drain(); ctx.inc(v.nop(), "v", 1)
            blk.vector(t3c)
            t3half_v.append(ctx.cnt["v"])
        vt3A, vt3B = t3half_v
        def dsm(t, rc=rc, par=par, ps=ps, w_vA=vt3A):
            ctx.wait(t, "v", w_vA)
            for h in range(NH):
                _li = t.matmul(ps[0][32 * (h % 4):32 * (h % 4) + 32, 64 * (h // 4):64 * (h // 4) + 64],
                               R.khat[:, 32 * h:32 * (h + 1)], R.ccv[par][:, 64 * h:64 * (h + 1)],
                               start=True, stop=True, tile_position=(0, 32 * (h % 4)))
            ctx.inc(_li, "t", 1)
        blk.tensor(dsm)
        t_dsm_prev = ctx.cnt["t"]
        st8.ccfree[par] = ctx.cnt["t"]
        def dse(s, rc=rc, ps=ps, w_t=t_dsm_prev):
            ctx.wait(s, "t", w_t)
            dst = strided_ap(R.dS[:], rc, [[64 * 32, 8], [32, 64]])
            s.activation(dst, ps[0][:].rearrange("p (j v) -> p j v", j=8), AF.Copy)
            s.drain(); ctx.inc(s.nop(), "s", 1)
        blk.scalar(dse)
        s_dse = ctx.cnt["s"]
        oim_t = []
        for n, bank, w_v, w_s in ((0, 1, vt3A, 0), (1, 2, vt3B, 0), (2, 3, vt3B, 0), (3, 0, 0, s_dse)):
            def oim(t, rc=rc, gi=gi, ps=ps, n=n, bank=bank, w_v=w_v, w_s=w_s):
                if w_v:
                    ctx.wait(t, "v", w_v)
                if w_s:
                    ctx.wait(t, "s", w_s)
                _li = t.matmul(ps[bank][:], P.prsh[:, 128 * gi:128 * (gi + 1)],
                               R.t3b[:, 512 * n:512 * (n + 1)], start=True, stop=True)
                ctx.inc(_li, "t", 1)
            blk.tensor(oim)
            oim_t.append(ctx.cnt["t"])
        def vac(v, rc=rc, gi=gi, ps=ps, oim_t=tuple(oim_t), sg_cnt=ctx.cnt["sg"]):
            if gi == 0:
                ctx.wait(v, "sg", sg_cnt)
            ctx.wait(v, "t", oim_t[0])
            if gi == 0:
                v.tensor_copy(R.oacc[:, 0:512], ps[1][:])
            else:
                v.tensor_tensor(R.oacc[:, 0:512], R.oacc[:, 0:512], ps[1][:], OP.add)
            v.drain()
            ctx.wait(v, "t", oim_t[2])
            src23 = (R.psB4 if ps is setB else P.psA4)[:, 1024:2048]
            if gi == 0:
                v.tensor_copy(R.oacc[:, 512:1536], src23)
            else:
                v.tensor_tensor(R.oacc[:, 512:1536], R.oacc[:, 512:1536], src23, OP.add)
            v.drain()
            ctx.wait(v, "t", oim_t[3])
            if gi == 0:
                v.tensor_copy(R.oacc[:, 1536:2048], ps[0][:])
            else:
                v.tensor_tensor(R.oacc[:, 1536:2048], R.oacc[:, 1536:2048], ps[0][:], OP.add)
            v.drain(); ctx.inc(v.nop(), "v", 1)
        blk.vector(vac)
        va3[par] = ctx.cnt["v"]
        if gi == 7:
            def ogs(e, rc=rc, w_v=va3[par]):
                ctx.wait(e, "sg")
                ctx.wait(e, "v", w_v)
                g = rc // 8
                ctx.inc(e.dma_start(out=dram["o_d"][t0 + 128 * g:t0 + 128 * (g + 1), :], in_=R.oacc[:]), "sg", 16)
            blk.gpsimd(ogs)
    if upto in ('rwkv_a', 'rwkv_b', 'rwkv_c', 'rwkv_d'):
        return
    def scn(v):
        ctx.wait(v, "s")
        ctx.wait(v, "sg")
        for mt in range(8):
            dc0 = P.Dcol[:, 32 * mt:32 * mt + 1].broadcast_to([128, 64])
            v.tensor_tensor(P.tB[:, 0:64], P.Scar[:, 64 * mt:64 * (mt + 1)], dc0, OP.mult)
            v.drain()
            c0 = strided_ap(R.dS[:], 64 * 32 * mt, [[32, 64]])
            v.tensor_tensor(c0, c0, P.tB[:, 0:64], OP.add)
            v.drain()
            d0v = strided_ap(P.D0[:], 32 * mt, [[0, 64], [1, 32]])
            v.tensor_copy(P.tA[:].rearrange("p (a b) -> p a b", a=64), d0v)
            v.drain()
            v.tensor_tensor_scan(R.dS[:, 2048 * mt:2048 * (mt + 1)], P.tA[:],
                                 R.dS[:, 2048 * mt:2048 * (mt + 1)], 0.0, OP.mult, OP.add)
            v.drain()
        ctx.inc(v.nop(), "v", 1)
    blk.vector(scn)
    if upto == 'rwkv_e':
        return
    scb_v = []
    oin_t = []
    oie_s = []
    def emit_scb(rc):
        par = rc % 2
        def scb(v, rc=rc, par=par):
            if rc == 0:
                ctx.wait(v, "v")
                v.tensor_copy(R.dsbf[par][:], P.Scar[:])
            else:
                if rc >= 2:
                    ctx.wait(v, "t", oin_t[rc - 2])
                v.tensor_copy(R.dsbf[par][:].rearrange("p (m v) -> p m v", m=8),
                              strided_ap(R.dS[:], rc - 1, [[2048, 8], [32, 64]]))
            v.drain(); ctx.inc(v.nop(), "v", 1)
        blk.vector(scb)
        scb_v.append(ctx.cnt["v"])
    def emit_oin(rc):
        par = rc % 2
        def oin(t, rc=rc, par=par):
            ctx.wait(t, "v", scb_v[rc])
            if rc >= 2:
                ctx.wait(t, "s", oie_s[rc - 2])
            for a4 in range(4):
                for j in range(8):
                    h = a4 + 4 * j
                    lhsT = P.rr[32 * a4:32 * a4 + 32, CH * (h // 4) + RC * rc:CH * (h // 4) + RC * (rc + 1)]
                    rhs = R.dsbf[par][32 * a4:32 * a4 + 32, 64 * (h // 4):64 * (h // 4) + 64]
                    _li = t.matmul(P.psA4[0:16, 64 * h:64 * (h + 1)] if par == 0 else
                                   R.psB4[32:48, 64 * h:64 * (h + 1)], lhsT, rhs,
                                   start=True, stop=True, tile_position=(32 * a4, 32 if par else 0))
            ctx.inc(_li, "t", 1)
        blk.tensor(oin)
        oin_t.append(ctx.cnt["t"])
    def emit_oie(rc):
        par = rc % 2
        def oie(s, rc=rc, par=par):
            ctx.wait(s, "t", oin_t[rc])
            ctx.wait(s, "sg")
            src_ = P.psA4[0:16, :] if par == 0 else R.psB4[32:48, :]
            s.activation(R.oint[:], src_, AF.Copy)
            s.drain(); ctx.inc(s.nop(), "s", 1)
        blk.scalar(oie)
        oie_s.append(ctx.cnt["s"])
    def emit_ois(rc):
        par = rc % 2
        tr = t0 + RC * rc
        def ois(e, rc=rc, par=par, tr=tr):
            ctx.wait(e, "sg")
            ctx.wait(e, "s", oie_s[rc])
            ctx.inc(e.dma_start(out=dram["ointer_d"][tr:tr + RC, :], in_=R.oint[:]), "sg", 16)
        blk.gpsimd(ois)
    for rc in range(NRC):
        emit_scb(rc)
        emit_oin(rc)
        if rc >= 1:
            emit_oie(rc - 1)
            emit_ois(rc - 1)
    emit_oie(NRC - 1)
    emit_ois(NRC - 1)
    if upto == 'rwkv_g':
        return
    def scu(v):
        ctx.wait(v, "t")
        for mt in range(8):
            src = strided_ap(R.dS[:], 2048 * mt + 31, [[32, 64]])
            v.tensor_copy(P.Scar[:, 64 * mt:64 * (mt + 1)], src)
        v.drain(); ctx.inc(v.nop(), "v", 1)
    blk.vector(scu)

def _emit_tail(nc, ctx, blk, dram, P, Q, st8, ci, t0, dumps):
    def tld(e):
        ctx.wait(e, "ls")
        ctx.inc(e.dma_start(out=Q.gnw[:], in_=dram["gnw_t"]), "ls", 16)
        ctx.inc(e.dma_start(out=Q.gnb[:], in_=dram["gnb_t"]), "ls", 16)
        ctx.inc(e.dma_start(out=Q.u128[:], in_=dram["u128"]), "ls", 16)
    blk.sync(tld)
    for nc2 in range(2):
        for kh in range(2):
            par = (nc2 * 2 + kh) % 2
            lcnt = _load_halfslab(ctx, blk, dram, Q.wslab[par], "Aff", nc2, kh, st8.slab_free[par], sem=f"w{par}")
            def vmm(t, nc2=nc2, kh=kh, par=par, lcnt=lcnt):
                ctx.wait(t, f"w{par}", lcnt)
                if kh == 0:
                    ctx.wait(t, "s")
                for tt in range(4):
                    psd = PSLIST(P)[tt]
                    for j in range(8):
                        kt = 8 * kh + j
                        _li = t.matmul(psd[:], P.xnT[:, CH * kt + 128 * tt:CH * kt + 128 * (tt + 1)],
                                 Q.wslab[par][:, 512 * j:512 * (j + 1)],
                                 start=(kh == 0 and j == 0), stop=(kh == 1 and j == 7))
                ctx.inc(_li, "t", 1)
            blk.tensor(vmm)
            st8.slab_free[par] = ctx.cnt["t"]
        for tt in range(4):
            def vsq(s, nc2=nc2, tt=tt):
                ctx.wait(s, "t")
                ctx.wait(s, "v")
                s.activation(P.tA[:, 0:512], PSLIST(P)[tt][:], AF.Square)
                s.activation(Q.vecsb[:, 1024 * tt + 512 * nc2:1024 * tt + 512 * (nc2 + 1)],
                             PSLIST(P)[tt][:], AF.Copy)
                s.drain(); ctx.inc(s.nop(), "s", 1)
            blk.scalar(vsq)
            def vre(v, nc2=nc2, tt=tt):
                ctx.wait(v, "s")
                src = strided_ap(P.tA[:], 0, [[1, 64], [64, 8]])
                v.tensor_reduce(strided_ap(P.gcol[:], 0, [[1, 64], [0, 1]]), src, AX, OP.add)
                v.drain()
                if nc2 == 0:
                    v.tensor_copy(Q.arro[:, 64 * tt:64 * (tt + 1)], P.gcol[:, 0:64])
                else:
                    v.tensor_tensor(Q.arro[:, 64 * tt:64 * (tt + 1)], Q.arro[:, 64 * tt:64 * (tt + 1)],
                                    P.gcol[:, 0:64], OP.add)
                v.drain(); ctx.inc(v.nop(), "v", 1)
            blk.vector(vre)
    for tt in range(4):
        def rts(v, tt=tt):
            ctx.wait(v, "v")
            a = Q.arro[:, 64 * tt:64 * (tt + 1)]
            ro = Q.rout[:, 64 * tt:64 * (tt + 1)]
            for it in range(4):
                v.tensor_reduce(P.colA[:, 0:1], a, AX, OP.max)
                v.drain()
                v.tensor_scalar(P.gcol[:, 0:64], a, P.colA[:, 0:1], None, OP.is_equal)
                v.drain()
                if it == 0:
                    v.tensor_copy(ro, P.gcol[:, 0:64])
                else:
                    v.tensor_tensor(ro, ro, P.gcol[:, 0:64], OP.add)
                v.scalar_tensor_tensor(a, P.gcol[:, 0:64], -1e30, a, OP.mult, OP.add)
                v.drain()
            for r16 in range(16):
                blkv = Q.vecsb[:, 1024 * tt + 64 * r16:1024 * tt + 64 * (r16 + 1)]
                v.tensor_tensor(P.tB[:, 0:64], blkv, ro, OP.mult)
                v.drain()
                v.tensor_reduce(Q.vsm[:, 16 * tt + r16:16 * tt + r16 + 1], P.tB[:, 0:64], AX, OP.add)
                v.drain()
            for r16 in range(16):
                v.tensor_scalar(Q.u2b[:, 1024 * tt + 64 * r16:1024 * tt + 64 * (r16 + 1)],
                                ro, Q.vsm[:, 16 * tt + r16:16 * tt + r16 + 1], 1.0 / 16.0,
                                OP.mult, OP.mult)
            v.drain(); ctx.inc(v.nop(), "v", 1)
        blk.vector(rts)
        def u2t(t, tt=tt):
            ctx.wait(t, "v")
            for j in range(8):
                _li = t.transpose(Q.pst[:, 128 * j:128 * (j + 1)],
                            Q.u2b[:, 1024 * tt + 128 * j:1024 * tt + 128 * (j + 1)], P.ident[:])
            ctx.inc(_li, "t", 1)
        blk.tensor(u2t)
        def u2e(v, tt=tt):
            ctx.wait(v, "t")
            v.tensor_copy(Q.u2T[:, 1024 * tt:1024 * (tt + 1)], Q.pst[:])
            v.drain(); ctx.inc(v.nop(), "v", 1)
        blk.vector(u2e)
    for tt in range(4):
        def oal(e, tt=tt):
            ctx.wait(e, "lc"); ctx.wait(e, "sg")
            ctx.wait(e, "v"); ctx.wait(e, "s"); ctx.wait(e, "t")
            r0 = t0 + 128 * tt
            ctx.inc(e.dma_start(out=P.tA[:], in_=dram["o_d"][r0:r0 + 128, :]), "lc", 16)
            ctx.inc(e.dma_start(out=P.tB[:], in_=dram["ointer_d"][r0:r0 + 128, :]), "lc", 16)
            ctx.inc(e.dma_start(out=P.bfA[:], in_=dram["vtok_d"][r0:r0 + 128, :]), "lc", 16)
            ctx.inc(e.dma_start(out=P.bfB[:], in_=dram["gtok_d"][r0:r0 + 128, :]), "lc", 16)
        blk.scalar(oal)
        lcc = ctx.cnt["lc"]
        def oas(v, tt=tt, lcc=lcc):
            ctx.wait(v, "lc", lcc)
            v.tensor_tensor(Q.obuf[:], P.tA[:], P.tB[:], OP.add)
            v.drain()
            dv = strided_ap(P.dbuf[:], 32 * tt, [[1, 32], [0, 64]])
            v.tensor_tensor(P.tB[:].rearrange("p (h v) -> p h v", h=32),
                            P.bfA[:].rearrange("p (h v) -> p h v", h=32), dv, OP.mult)
            v.drain()
            v.tensor_tensor(Q.obuf[:], Q.obuf[:], P.tB[:], OP.add)
            v.drain()
            v.tensor_reduce(strided_ap(P.gcol[:], 0, [[1, 32], [0, 1]]),
                            Q.obuf[:].rearrange("p (h v) -> p h v", h=32), AX, OP.add)
            v.drain()
            v.tensor_scalar(P.gcol[:, 0:32], P.gcol[:, 0:32], 1.0 / 64, None, OP.mult)
            v.drain()
            ctx.inc(v.nop(), "v", 1)
        blk.vector(oas)
        def osq(s, tt=tt):
            ctx.wait(s, "v")
            s.activation(P.tA[:], Q.obuf[:], AF.Square)
            s.drain(); ctx.inc(s.nop(), "s", 1)
        blk.scalar(osq)
        def ost(v, tt=tt):
            ctx.wait(v, "s")
            v.tensor_reduce(strided_ap(P.gcol[:], 32, [[1, 32], [0, 1]]),
                            P.tA[:].rearrange("p (h v) -> p h v", h=32), AX, OP.add)
            v.drain()
            v.tensor_scalar(P.gcol[:, 32:64], P.gcol[:, 32:64], 1.0 / 64, None, OP.mult)
            v.drain()
            v.tensor_tensor(P.gcol[:, 64:96], P.gcol[:, 0:32], P.gcol[:, 0:32], OP.mult)
            v.drain()
            v.tensor_tensor(P.gcol[:, 32:64], P.gcol[:, 32:64], P.gcol[:, 64:96], OP.subtract)
            v.drain()
            v.tensor_scalar(P.gcol[:, 32:64], P.gcol[:, 32:64], GN_EPS, None, OP.add)
            v.drain(); ctx.inc(v.nop(), "v", 1)
        blk.vector(ost)
        def osr(s, tt=tt):
            ctx.wait(s, "v")
            s.activation(P.gcol[:, 64:96], P.gcol[:, 32:64], AF.Sqrt)
            s.drain(); ctx.inc(s.nop(), "s", 1)
        blk.scalar(osr)
        def onr(v, tt=tt):
            ctx.wait(v, "s")
            ctx.wait(v, "ls")
            v.reciprocal(P.gcol[:, 96:128], P.gcol[:, 64:96])
            v.drain()
            mb = strided_ap(P.gcol[:], 0, [[1, 32], [0, 64]])
            rb = strided_ap(P.gcol[:], 96, [[1, 32], [0, 64]])
            ov = Q.obuf[:].rearrange("p (h v) -> p h v", h=32)
            v.tensor_tensor(ov, ov, mb, OP.subtract)
            v.drain()
            v.tensor_tensor(ov, ov, rb, OP.mult)
            v.drain()
            v.tensor_tensor(Q.obuf[:], Q.obuf[:], Q.gnw[:], OP.mult)
            v.drain()
            v.tensor_tensor(Q.obuf[:], Q.obuf[:], Q.gnb[:], OP.add)
            v.drain(); ctx.inc(v.nop(), "v", 1)
        blk.vector(onr)
        def osl(s, tt=tt):
            ctx.wait(s, "v")
            s.activation(P.tB[:], P.bfB[:], AF.Silu)
            s.drain(); ctx.inc(s.nop(), "s", 1)
        blk.scalar(osl)
        def oga(v, tt=tt):
            ctx.wait(v, "s")
            v.tensor_tensor(P.bfA[:], Q.obuf[:], P.tB[:], OP.mult)
            v.drain(); ctx.inc(v.nop(), "v", 1)
        blk.vector(oga)
        if "og" in dumps:
            def ogd(e, tt=tt):
                ctx.wait(e, "v")
                ctx.inc(e.dma_start(out=dram["dump_og"][t0 + 128 * tt:t0 + 128 * (tt + 1), :],
                                    in_=P.bfA[:]), "sg", 16)
            blk.gpsimd(ogd)
        for fh in range(2):
            def ogt(t, tt=tt, fh=fh):
                ctx.wait(t, "v")
                for j in range(8):
                    ft = 8 * fh + j
                    _li = t.transpose(Q.pst[:, 128 * j:128 * (j + 1)], P.bfA[:, 128 * ft:128 * (ft + 1)], P.ident[:])
                ctx.inc(_li, "t", 1)
            blk.tensor(ogt)
            def oge2(v, tt=tt, fh=fh):
                ctx.wait(v, "t")
                v.tensor_copy(Q.ogT[:, 2048 * tt + 1024 * fh:2048 * tt + 1024 * (fh + 1)], Q.pst[:])
                v.drain(); ctx.inc(v.nop(), "v", 1)
            blk.vector(oge2)
    for nck in range(4):
        for kh in range(2):
            par = (nck * 2 + kh) % 2
            lcnt = _load_halfslab(ctx, blk, dram, Q.wslab[par], "Wo", nck, kh, st8.slab_free[par], sem=f"w{par}")
            def smm(t, nck=nck, kh=kh, par=par, lcnt=lcnt):
                ctx.wait(t, "v")
                ctx.wait(t, f"w{par}", lcnt)
                if kh == 0:
                    ctx.wait(t, "s")
                for tt in range(4):
                    psd = PSLIST(P)[tt]
                    for j in range(8):
                        kt = 8 * kh + j
                        _li = t.matmul(psd[:], Q.ogT[:, 2048 * tt + 128 * kt:2048 * tt + 128 * (kt + 1)],
                                 Q.wslab[par][:, 512 * j:512 * (j + 1)],
                                 start=(kh == 0 and j == 0), stop=(kh == 1 and j == 7))
                ctx.inc(_li, "t", 1)
            blk.tensor(smm)
            st8.slab_free[par] = ctx.cnt["t"]
        def sev(s, nck=nck):
            ctx.wait(s, "t")
            ctx.wait(s, "sg")
            for tt in range(4):
                s.activation(Q.obuf[:, 512 * tt:512 * (tt + 1)], PSLIST(P)[tt][:], AF.Copy)
            s.drain(); ctx.inc(s.nop(), "s", 1)
        blk.scalar(sev)
        if "s" in dumps:
            def sdp(e, nck=nck):
                ctx.wait(e, "s")
                for tt in range(4):
                    ctx.inc(e.dma_start(out=dram["dump_s"][t0 + 128 * tt:t0 + 128 * (tt + 1),
                                                           512 * nck:512 * (nck + 1)],
                                        in_=Q.obuf[:, 512 * tt:512 * (tt + 1)]), "sg", 16)
            blk.gpsimd(sdp)
        for kh in range(2):
            par = (nck * 2 + kh) % 2
            lcnt = _load_halfslab(ctx, blk, dram, Q.wslab[par], "Wff", nck, kh, st8.slab_free[par], sem=f"w{par}")
            def fmm(t, nck=nck, kh=kh, par=par, lcnt=lcnt):
                ctx.wait(t, f"w{par}", lcnt)
                if kh == 0:
                    ctx.wait(t, "s")
                for tt in range(4):
                    psd = PSLIST(P)[tt]
                    for j in range(8):
                        kt = 8 * kh + j
                        _li = t.matmul(psd[:], P.xnT[:, CH * kt + 128 * tt:CH * kt + 128 * (tt + 1)],
                                 Q.wslab[par][:, 512 * j:512 * (j + 1)],
                                 start=(kh == 0 and j == 0), stop=False)
                ctx.inc(_li, "t", 1)
            blk.tensor(fmm)
            st8.slab_free[par] = ctx.cnt["t"]
        for kh in range(2):
            par = (nck * 2 + kh) % 2
            lcnt = _load_halfslab(ctx, blk, dram, Q.wslab[par], "Bff", nck, kh, st8.slab_free[par], kts=4, sem=f"w{par}")
            def bmm(t, nck=nck, kh=kh, par=par, lcnt=lcnt):
                ctx.wait(t, f"w{par}", lcnt)
                for tt in range(4):
                    psd = PSLIST(P)[tt]
                    for j in range(4):
                        kt = 4 * kh + j
                        _li = t.matmul(psd[:], Q.u2T[:, 1024 * tt + 128 * kt:1024 * tt + 128 * (kt + 1)],
                                 Q.wslab[par][:, 512 * j:512 * (j + 1)],
                                 start=False, stop=False)
                ctx.inc(_li, "t", 1)
            blk.tensor(bmm)
            st8.slab_free[par] = ctx.cnt["t"]
        def xbias(t, nck=nck):
            for tt in range(4):
                psd = PSLIST(P)[tt]
                _li = t.matmul(psd[:], P.ones_rf[32:33, :], P.rows[32:33, 512 * nck:512 * (nck + 1)],
                         start=False, stop=True)
            ctx.inc(_li, "t", 1)
        blk.tensor(xbias)
        def x0l(e, nck=nck):
            ctx.wait(e, "lc")
            ctx.wait(e, "v"); ctx.wait(e, "s")
            for tt in range(4):
                ctx.inc(e.dma_start(out=P.tB[:, 512 * tt:512 * (tt + 1)],
                                    in_=dram["x0"][t0 + 128 * tt:t0 + 128 * (tt + 1),
                                                   512 * nck:512 * (nck + 1)]), "lc", 16)
        blk.scalar(x0l)
        lcc = ctx.cnt["lc"]
        def gsl(s, nck=nck):
            ctx.wait(s, "t")
            ctx.wait(s, "sg")
            for tt in range(4):
                s.activation(P.tA[:, 512 * tt:512 * (tt + 1)], PSLIST(P)[tt][:], AF.Silu)
            s.drain(); ctx.inc(s.nop(), "s", 1)
        blk.scalar(gsl)
        if "gff" in dumps:
            def gfd(e, nck=nck):
                ctx.wait(e, "s")
                for tt in range(4):
                    ctx.inc(e.dma_start(out=dram["dump_gff"][t0 + 128 * tt:t0 + 128 * (tt + 1),
                                                             512 * nck:512 * (nck + 1)],
                                        in_=P.tA[:, 512 * tt:512 * (tt + 1)]), "sg", 16)
            blk.gpsimd(gfd)
        def x1c(v, nck=nck, lcc=lcc):
            ctx.wait(v, "s")
            ctx.wait(v, "lc", lcc)
            ctx.wait(v, "sg")
            v.tensor_tensor(P.tA[:], Q.obuf[:], P.tA[:], OP.mult)
            v.drain()
            v.tensor_tensor(P.tA[:], P.tA[:], P.tB[:], OP.add)
            v.drain(); ctx.inc(v.nop(), "v", 1)
        blk.vector(x1c)
        def x1s(e, nck=nck):
            ctx.wait(e, "sg")
            ctx.wait(e, "v")
            for tt in range(4):
                ctx.inc(e.dma_start(out=dram["x1_d"][t0 + 128 * tt:t0 + 128 * (tt + 1),
                                                     512 * nck:512 * (nck + 1)],
                                    in_=P.tA[:, 512 * tt:512 * (tt + 1)]), "sg", 16)
        blk.gpsimd(x1s)
    for tt in range(4):
        def x1l(e, tt=tt):
            ctx.wait(e, "lc")
            ctx.wait(e, "sg")
            ctx.wait(e, "v"); ctx.wait(e, "s"); ctx.wait(e, "t")
            ctx.inc(e.dma_start(out=P.tA[:], in_=dram["x1_d"][t0 + 128 * tt:t0 + 128 * (tt + 1), :]), "lc", 16)
        blk.scalar(x1l)
        lcc = ctx.cnt["lc"]
        def l1v(v, tt=tt, lcc=lcc):
            ctx.wait(v, "lc", lcc)
            v.tensor_reduce(P.colA[:, 0:1], P.tA[:], AX, OP.add)
            v.drain(); ctx.inc(v.nop(), "v", 1)
        blk.vector(l1v)
        def l1s(s, tt=tt):
            ctx.wait(s, "lc", lcc)
            ctx.wait(s, "v")
            s.activation(P.tB[:], P.tA[:], AF.Square, accum_out=P.colB[:, 0:1])
            s.drain(); ctx.inc(s.nop(), "s", 1)
        blk.scalar(l1s)
        def l1v2(v, tt=tt):
            ctx.wait(v, "s")
            v.tensor_scalar(P.colA[:, 1:2], P.colA[:, 0:1], 1.0 / D, None, OP.mult)
            v.tensor_scalar(P.colB[:, 1:2], P.colB[:, 0:1], 1.0 / D, None, OP.mult)
            v.drain()
            v.tensor_scalar(P.colA[:, 2:3], P.colA[:, 1:2], P.colA[:, 1:2], -1.0, OP.mult, OP.mult)
            v.drain()
            v.tensor_tensor(P.colB[:, 2:3], P.colB[:, 1:2], P.colA[:, 2:3], OP.add)
            v.drain()
            v.tensor_scalar(P.colB[:, 4:5], P.colB[:, 2:3], LN_EPS, None, OP.add)
            v.drain(); ctx.inc(v.nop(), "v", 1)
        blk.vector(l1v2)
        def l1s2(s, tt=tt):
            ctx.wait(s, "v")
            s.activation(P.colB[:, 3:4], P.colB[:, 4:5], AF.Sqrt)
            s.drain(); ctx.inc(s.nop(), "s", 1)
        blk.scalar(l1s2)
        def l1v3(v, tt=tt):
            ctx.wait(v, "s")
            ctx.wait(v, "sg")
            v.reciprocal(P.colA[:, 3:4], P.colB[:, 3:4])
            v.drain()
            v.tensor_scalar(P.colA[:, 4:5], P.colA[:, 1:2], P.colA[:, 3:4], -1.0, OP.mult, OP.mult)
            v.drain()
            v.tensor_scalar(P.bfA[:], P.tA[:], P.colA[:, 3:4], P.colA[:, 4:5], OP.mult, OP.add)
            v.drain()
            pin = strided_ap(P.bfA[:], 0, [[128, 16], [1, 2], [2, 64]])
            pout = P.bfB[:].rearrange("p (b s j) -> p b s j", b=16, s=2)
            v.tensor_copy(pout, pin)
            v.drain(); ctx.inc(v.nop(), "v", 1)
        blk.vector(l1v3)
        for fh in range(2):
            def xnt(t, tt=tt, fh=fh):
                ctx.wait(t, "v")
                for j in range(8):
                    ft = 8 * fh + j
                    _li = t.transpose(Q.pst[:, 128 * j:128 * (j + 1)], P.bfB[:, 128 * ft:128 * (ft + 1)], P.ident[:])
                ctx.inc(_li, "t", 1)
            blk.tensor(xnt)
            def xne(v, tt=tt, fh=fh):
                ctx.wait(v, "t")
                dst = strided_ap(Q.xn1T[:], CH * 8 * fh + 128 * tt, [[CH, 8], [1, 128]])
                v.tensor_copy(dst, Q.pst[:].rearrange("p (j m) -> p j m", j=8))
                v.drain(); ctx.inc(v.nop(), "v", 1)
            blk.vector(xne)
    def xns(e):
        ctx.wait(e, "sg")
        ctx.wait(e, "v")
        for ft in range(16):
            ctx.inc(e.dma_start(out=dram["xn1T_d"][128 * ft:128 * (ft + 1), t0:t0 + CH],
                                in_=Q.xn1T[:, CH * ft:CH * (ft + 1)]), "sg", 16)
    blk.gpsimd(xns)
    for part, dn, bias in ((0, "qT_d", P.bq_c), (1, "kT_d", P.bk_c)):
        for mt in range(16):
            par = mt % 2
            def qld(e, part=part, mt=mt, par=par):
                ctx.wait(e, f"q{par}")
                if st8.lslab_free[par] > 0:
                    ctx.wait(e, "t", st8.lslab_free[par])
                src = dram["Wqkv"].rearrange("(kt p) m -> p kt m", p=128)[:, :,
                                                                         2048 * part + 128 * mt:2048 * part + 128 * (mt + 1)]
                ctx.inc(e.dma_start(out=Q.lslab[par][:].rearrange("p (kt m) -> p kt m", kt=16),
                                    in_=src, allow_slow_non_contiguous=True), f"q{par}", 16)
            blk.sync(qld)
            lcnt = ctx.cnt[f"q{par}"]
            def qmm(t, part=part, mt=mt, par=par, lcnt=lcnt):
                ctx.wait(t, f"q{par}", lcnt)
                ctx.wait(t, "v")
                ctx.wait(t, "s")
                psd = PSLIST(P)[mt % 2]
                for kt in range(16):
                    _li = t.matmul(psd[:], Q.lslab[par][:, 128 * kt:128 * (kt + 1)],
                             Q.xn1T[:, CH * kt:CH * (kt + 1)], start=(kt == 0), stop=(kt == 15))
                ctx.inc(_li, "t", 1)
            blk.tensor(qmm)
            st8.lslab_free[par] = ctx.cnt["t"]
            def qev(s, part=part, mt=mt, bias=bias):
                ctx.wait(s, "t")
                ctx.wait(s, "sg")
                psd = PSLIST(P)[mt % 2]
                s.activation(P.bfA[:, 0:512], psd[:], AF.Identity, bias=bias[:, mt:mt + 1])
                s.drain(); ctx.inc(s.nop(), "s", 1)
            blk.scalar(qev)
            def qst(e, part=part, mt=mt, dn=dn):
                ctx.wait(e, "sg")
                ctx.wait(e, "s")
                ctx.inc(e.dma_start(out=dram[dn][128 * mt:128 * (mt + 1), t0:t0 + CH],
                                    in_=P.bfA[:, 0:512]), "sg", 16)
            blk.gpsimd(qst)
    for nck in range(4):
        for kh in range(2):
            par = (nck * 2 + kh) % 2
            lcnt = _load_halfslab(ctx, blk, dram, Q.wslab[par], "Wqkv", 8 + nck, kh, st8.slab_free[par], sem=f"w{par}")
            def v1m(t, nck=nck, kh=kh, par=par, lcnt=lcnt):
                ctx.wait(t, f"w{par}", lcnt)
                if kh == 0:
                    ctx.wait(t, "s")
                for tt in range(4):
                    psd = PSLIST(P)[tt]
                    for j in range(8):
                        kt = 8 * kh + j
                        _li = t.matmul(psd[:], Q.xn1T[:, CH * kt + 128 * tt:CH * kt + 128 * (tt + 1)],
                                 Q.wslab[par][:, 512 * j:512 * (j + 1)],
                                 start=(kh == 0 and j == 0), stop=False)
                ctx.inc(_li, "t", 1)
            blk.tensor(v1m)
            st8.slab_free[par] = ctx.cnt["t"]
        def v1b(t, nck=nck):
            for tt in range(4):
                _li = t.matmul(PSLIST(P)[tt][:], P.ones_rf[64:65, :], P.rows[64:65, 512 * nck:512 * (nck + 1)],
                         start=False, stop=True)
            ctx.inc(_li, "t", 1)
        blk.tensor(v1b)
        def v1e(s, nck=nck):
            ctx.wait(s, "t")
            ctx.wait(s, "sg")
            for tt in range(4):
                s.activation(P.bfA[:, 512 * tt:512 * (tt + 1)], PSLIST(P)[tt][:], AF.Copy)
            s.drain(); ctx.inc(s.nop(), "s", 1)
        blk.scalar(v1e)
        def v1s(e, nck=nck):
            ctx.wait(e, "sg")
            ctx.wait(e, "s")
            for tt in range(4):
                ctx.inc(e.dma_start(out=dram["v1tok_d"][t0 + 128 * tt:t0 + 128 * (tt + 1),
                                                        512 * nck:512 * (nck + 1)],
                                    in_=P.bfA[:, 512 * tt:512 * (tt + 1)]), "sg", 16)
        blk.gpsimd(v1s)


ATT_SCALE = 1.0 / np.sqrt(128.0)

def build_stageB(scratch_out=False, n_heads=16):
    nc = bass.Bass("TRN2", target_bir_lowering=False)
    dram = {}
    dram["qT_d"] = nc.dram_tensor("qT_d", [D, T], BF16, kind="ExternalInput").ap()
    dram["kT_d"] = nc.dram_tensor("kT_d", [D, T], BF16, kind="ExternalInput").ap()
    dram["v1tok_d"] = nc.dram_tensor("v1tok_d", [T, D], BF16, kind="ExternalInput").ap()
    dram["Wout"] = nc.dram_tensor("Wout_b", [D, D], BF16, kind="ExternalInput").ap()
    dram["bout_row"] = nc.dram_tensor("bout_row", [1, 2048], F32, kind="ExternalInput").ap()
    dram["ones_row_f32"] = nc.dram_tensor("ones_row_f32", [128, 128], F32, kind="ExternalInput").ap()
    dram["oT_d"] = nc.dram_tensor("oT_d", [D, T], BF16,
                                  kind="ExternalOutput" if scratch_out else "Internal").ap()
    dram["s1_d"] = nc.dram_tensor("s1_d", [T, D], F32, kind="ExternalOutput").ap()

    with ExitStack() as stack:
        ctx = Ctx(nc, stack)
        blk = stack.enter_context(nc.Block())
        _stageB_emit(nc, ctx, blk, stack, dram, n_heads)
    return nc

def _stageB_emit(nc, ctx, blk, stack0, dram, n_heads=16):
    with ExitStack() as stack:
        sb = lambda name, shape, dt=F32: stack.enter_context(nc.sbuf_tensor(name, shape, dt))
        qh = [sb(f"qh{i}", [128, T], BF16) for i in range(2)]
        kh = [sb(f"kh{i}", [128, T], BF16) for i in range(2)]
        vh = [sb(f"vh{i}", [128, 32 * 128], BF16) for i in range(2)]
        pt = [sb(f"pt{i}", [128, 1024], BF16) for i in range(2)]
        oTh = sb("oTh", [128, T], BF16)
        den = sb("den", [1, 512])
        tA = sb("tAb", [128, 512])
        ones_rf = sb("ones_rfb", [128, 128])
        onec = sb("onec", [128, 1], BF16)
        wout = sb("woutb", [128, 16 * 2048], BF16)
        brow = sb("browb", [1, 2048])
        oslab = [sb(f"oslab{i}", [128, 16 * 128], BF16) for i in range(2)]
        sev = sb("sevb", [128, 2048])
        psS4 = stack.enter_context(nc.psum_tensor("psS4", [128, 2048], F32))
        psS = [psS4[:, 512 * i:512 * (i + 1)] for i in range(4)]
        psO = stack.enter_context(nc.psum_tensor("psO", [128, 512], F32))
        psD = stack.enter_context(nc.psum_tensor("psD", [128, 512], F32))
        psB = stack.enter_context(nc.psum_tensor("psB", [128, 512], F32))
        for s_ in ("lh", "sg", "t", "v", "s", "q0", "q1", "lh1"):
            ctx.sem(s_)

        def l0(e):
            ctx.wait(e, "lh")
            ctx.inc(e.dma_start(out=ones_rf[:], in_=dram["ones_row_f32"]), "lh", 16)
            ctx.inc(e.dma_start(out=brow[:], in_=dram["bout_row"]), "lh", 16)
            ctx.inc(e.dma_start(out=wout[:].rearrange("p (kt n) -> p kt n", kt=16),
                                in_=dram["Wout"].rearrange("(kt p) n -> p kt n", p=128)), "lh", 16)
        blk.sync(l0)
        def i0(v):
            ctx.wait(v, "v")
            v.memset(onec[:], 1.0)
            v.drain()
            ctx.inc(v.nop(), "v", 1)
        blk.vector(i0)
        v_base = ctx.cnt["v"]

        head_done_t = []
        lh_cnt = [0, 0]
        head_prev_s = 0
        for h in range(n_heads):
            par = h % 2
            def lhd(e, h=h, par=par):
                ctx.wait(e, "lh1" if par else "lh")
                if h >= 2:
                    ctx.wait(e, "t", head_done_t[h - 2])
                sem = "lh1" if par else "lh"
                ctx.inc(e.dma_start(out=qh[par][:], in_=dram["qT_d"][128 * h:128 * (h + 1), :]), sem, 16)
                ctx.inc(e.dma_start(out=kh[par][:], in_=dram["kT_d"][128 * h:128 * (h + 1), :]), sem, 16)
                ctx.inc(e.dma_start(out=vh[par][:].rearrange("p (tt c) -> p tt c", tt=32),
                                    in_=dram["v1tok_d"][:, 128 * h:128 * (h + 1)]
                                    .rearrange("(tt p) c -> p tt c", p=128)), sem, 16)
            blk.sync(lhd)
            lh_cnt[par] = ctx.cnt["lh1" if par else "lh"]
            def qsc(v, h=h, par=par, lc=lh_cnt[par]):
                ctx.wait(v, "lh1" if par else "lh", lc)
                v.tensor_scalar(qh[par][:], qh[par][:], float(ATT_SCALE), None, OP.mult)
                v.drain(); ctx.inc(v.nop(), "v", 1)
            blk.vector(qsc)
            q_scaled = ctx.cnt["v"]
            s_exp = []
            t_av = []
            qcb = {"s": head_prev_s, "v": 0}
            for qc in range(8):
                def emit_smm(kt):
                    kb = (kt // 2) % 2
                    gi = 32 * qc + kt
                    def smm(t, qc=qc, kt=kt, kb=kb, gi=gi, par=par):
                        if qc == 0 and kt == 0:
                            ctx.wait(t, "lh1" if par else "lh", lh_cnt[par])
                            ctx.wait(t, "v", q_scaled)
                        if gi >= 4:
                            ctx.wait(t, "s", s_exp[(gi - 4) // 2])
                        elif h > 0:
                            ctx.wait(t, "s", head_prev_s)
                        _li = t.matmul(psS[2 * kb][:], kh[par][:, 128 * kt:128 * (kt + 1)],
                                       qh[par][:, 512 * qc:512 * (qc + 1)], start=True, stop=True)
                        _li = t.matmul(psS[2 * kb + 1][:], kh[par][:, 128 * (kt + 1):128 * (kt + 2)],
                                       qh[par][:, 512 * qc:512 * (qc + 1)], start=True, stop=True)
                        ctx.inc(_li, "t", 1)
                    blk.tensor(smm)
                    t_s = ctx.cnt["t"]
                    def sex(s, qc=qc, kt=kt, kb=kb, gi=gi, t_s=t_s):
                        ctx.wait(s, "t", t_s)
                        if gi >= 4:
                            ctx.wait(s, "t", t_av[gi - 4 + 1])
                        s.activation(pt[kb][:], psS4[:, 1024 * kb:1024 * (kb + 1)], AF.Exp)
                        s.drain()
                        ctx.inc(s.nop(), "s", 1)
                    blk.scalar(sex)
                    s_exp.append(ctx.cnt["s"])
                def emit_avm(kt):
                    kb = (kt // 2) % 2
                    hw = kt % 2
                    gi = 32 * qc + kt
                    def avm(t, qc=qc, kt=kt, kb=kb, hw=hw, gi=gi, par=par, qcb=qcb):
                        ctx.wait(t, "s", s_exp[gi // 2])
                        if kt == 0:
                            ctx.wait(t, "v", qcb["v"])
                        ptv = pt[kb][:, 512 * hw:512 * (hw + 1)]
                        _li = t.matmul(psO[:], vh[par][:, 128 * kt:128 * (kt + 1)], ptv,
                                       start=(kt == 0), stop=(kt == 31))
                        _li = t.matmul(psD[0:1, :], onec[:], ptv,
                                       start=(kt == 0), stop=(kt == 31))
                        ctx.inc(_li, "t", 1)
                    blk.tensor(avm)
                    t_av.append(ctx.cnt["t"])
                for kp in range(16):
                    if kp >= 2:
                        emit_avm(2 * kp - 4)
                        emit_avm(2 * kp - 3)
                    emit_smm(2 * kp)
                for kt in (28, 29, 30, 31):
                    emit_avm(kt)
                t_q = ctx.cnt["t"]
                def nrm(v, h=h, qc=qc, t_q=t_q):
                    ctx.wait(v, "t", t_q)
                    v.tensor_copy(den[:], psD[0:1, :])
                    v.drain()
                    v.reciprocal(den[:], den[:])
                    v.drain()
                    ctx.inc(v.nop(), "v", 1)
                blk.vector(nrm)
                v_n = ctx.cnt["v"]
                def bcm(t, h=h, qc=qc, v_n=v_n):
                    ctx.wait(t, "v", v_n)
                    _li = t.matmul(psB[:], ones_rf[0:1, :], den[:], start=True, stop=True)
                    ctx.inc(_li, "t", 1)
                blk.tensor(bcm)
                t_b = ctx.cnt["t"]
                def onm(v, h=h, qc=qc, t_b=t_b):
                    ctx.wait(v, "t", t_b)
                    if qc == 0:
                        ctx.wait(v, "sg")
                    v.tensor_copy(tA[:], psO[:])
                    v.drain()
                    v.tensor_tensor(oTh[:, 512 * qc:512 * (qc + 1)], tA[:], psB[:], OP.mult)
                    v.drain()
                    ctx.inc(v.nop(), "v", 1)
                blk.vector(onm)
                qcb["v"] = ctx.cnt["v"]
            head_prev_s = ctx.cnt["s"]
            head_done_t.append(ctx.cnt["t"])
            def sto(e, h=h):
                ctx.wait(e, "sg")
                ctx.wait(e, "v")
                ctx.inc(e.dma_start(out=dram["oT_d"][128 * h:128 * (h + 1), :], in_=oTh[:]), "sg", 16)
            blk.gpsimd(sto)

        for tt in range(32):
            par = tt % 2
            def olb(e, tt=tt, par=par):
                ctx.wait(e, f"q{par}")
                if tt >= 2:
                    ctx.wait(e, "t")
                if tt == 0:
                    ctx.wait(e, "sg")
                ctx.inc(e.dma_start(out=oslab[par][:].rearrange("p (kt m) -> p kt m", kt=16),
                                    in_=dram["oT_d"].rearrange("(kt p) m -> p kt m", p=128)[:, :, 128 * tt:128 * (tt + 1)]),
                        f"q{par}", 16)
            blk.sync(olb)
            qcn = ctx.cnt[f"q{par}"]
            def omm(t, tt=tt, par=par, qcn=qcn):
                ctx.wait(t, f"q{par}", qcn)
                ctx.wait(t, "s")
                for nck in range(4):
                    psd = (psS[0], psS[1], psO, psD)[nck]
                    for kt in range(16):
                        _li = t.matmul(psd[:], oslab[par][:, 128 * kt:128 * (kt + 1)],
                                       wout[:, 2048 * kt + 512 * nck:2048 * kt + 512 * (nck + 1)],
                                       start=(kt == 0), stop=False)
                    _li = t.matmul(psd[:], ones_rf[0:1, :], brow[:, 512 * nck:512 * (nck + 1)],
                                   start=False, stop=True)
                ctx.inc(_li, "t", 1)
            blk.tensor(omm)
            def ose(s, tt=tt):
                ctx.wait(s, "t")
                ctx.wait(s, "sg")
                for nck in range(4):
                    psd = (psS[0], psS[1], psO, psD)[nck]
                    s.activation(sev[:, 512 * nck:512 * (nck + 1)], psd[:], AF.Copy)
                s.drain(); ctx.inc(s.nop(), "s", 1)
            blk.scalar(ose)
            def oss(e, tt=tt):
                ctx.wait(e, "sg")
                ctx.wait(e, "s")
                ctx.inc(e.dma_start(out=dram["s1_d"][128 * tt:128 * (tt + 1), :], in_=sev[:]), "sg", 16)
            blk.gpsimd(oss)
        def fin(e):
            ctx.wait(e, "sg")
        blk.gpsimd(fin)


def host_scale_expand_gu(sT):
    g = np.arange(128) // 16
    A = sT.reshape(8, 8, 16384)
    A = A.transpose(1, 0, 2)
    return np.ascontiguousarray(A[g])

def _alloc_dequant(nc, sc, width, name=""):
    Pq = SimpleNamespace()
    sb = lambda n, shape, dt: sc.enter_context(nc.sbuf_tensor(n + name, shape, dt))
    Pq.dq_mi = [sb(f"dq_mi{i}", [128, width], I32) for i in range(2)]
    Pq.dq_bi = [sb(f"dq_bi{i}", [128, width], I32) for i in range(2)]
    for n in ("mb", "fb", "r1", "r2", "sv"):
        setattr(Pq, "dq_" + n, [sb(f"dq_{n}{i}", [128, width], BF16) for i in range(2)])
    for n in ("t", "mag"):
        setattr(Pq, "dq_" + n, sb("dq_" + n, [128, width], BF16))
    Pq.dq_q = Pq.dq_t
    Pq.dq_one = sb("dq_one", [128, 1], F32)
    Pq.dq_m4 = sb("dq_m4", [128, 1], F32)
    Pq.dq_m6 = sb("dq_m6", [128, 1], F32)
    return Pq

def _emit_dq_init(ctx, blk, Pq):
    def ini(v):
        v.memset(Pq.dq_one[:], 1.0)
        v.memset(Pq.dq_m4[:], -4.0)
        v.memset(Pq.dq_m6[:], -6.0)
        v.drain(); ctx.inc(v.nop(), "v", 1)
    blk.vector(ini)

def _emit_dequant_stream(nc, ctx, blk, Pq, pk, shift, scl, width, lc_cnt, dqs):
    st = 0 if shift == 0 else 1
    def gp(g, shift=shift, st=st, w_s=dqs["s" + str(st)], lc_cnt=lc_cnt):
        ctx.wait(g, "lc", lc_cnt)
        ctx.wait(g, "s", w_s)
        if shift == 0:
            g.tensor_scalar(Pq.dq_mi[st][:, 0:width], pk[:, 0:width], 7, None, OP.bitwise_and)
            g.tensor_scalar(Pq.dq_bi[st][:, 0:width], pk[:, 0:width], 3, 1, OP.logical_shift_right, OP.bitwise_and)
        else:
            g.tensor_scalar(Pq.dq_mi[st][:, 0:width], pk[:, 0:width], 4, 7, OP.logical_shift_right, OP.bitwise_and)
            g.tensor_scalar(Pq.dq_bi[st][:, 0:width], pk[:, 0:width], 7, 1, OP.logical_shift_right, OP.bitwise_and)
        g.drain()
        ctx.inc(g.nop(), "g", 1)
    blk.vector(gp)
    dqs["g" + str(st)] = ctx.cnt["g"]
    def sc_(s, st=st, w_g=dqs["g" + str(st)], w_v=dqs["v" + str(st)]):
        ctx.wait(s, "g", w_g)
        ctx.wait(s, "v", w_v)
        s.activation(Pq.dq_mb[st][:, 0:width], Pq.dq_mi[st][:, 0:width], AF.Copy)
        s.activation(Pq.dq_fb[st][:, 0:width], Pq.dq_bi[st][:, 0:width], AF.Identity, scale=-2.0, bias=Pq.dq_one[:, 0:1])
        s.activation(Pq.dq_r1[st][:, 0:width], Pq.dq_mi[st][:, 0:width], AF.Relu, bias=Pq.dq_m4[:, 0:1])
        s.activation(Pq.dq_r2[st][:, 0:width], Pq.dq_mi[st][:, 0:width], AF.Relu, bias=Pq.dq_m6[:, 0:1])
        s.drain()
        ctx.inc(s.nop(), "s", 1)
    blk.scalar(sc_)
    dqs["s" + str(st)] = ctx.cnt["s"]
    def vv(v, st=st, w_s=dqs["s" + str(st)], w_free_v=dqs["svfree_v" + str(st)],
           w_free_sg=dqs["svfree_sg" + str(st)], w_free_t=dqs["svfree_t" + str(st)]):
        ctx.wait(v, "s", w_s)
        if w_free_sg:
            ctx.wait(v, "sg", w_free_sg)
        if w_free_v:
            ctx.wait(v, "v", w_free_v)
        if w_free_t:
            ctx.wait(v, "t", w_free_t)
        v.tensor_tensor(Pq.dq_t[:, 0:width], Pq.dq_mb[st][:, 0:width], Pq.dq_r1[st][:, 0:width], OP.add)
        v.drain()
        v.scalar_tensor_tensor(Pq.dq_mag[:, 0:width], Pq.dq_t[:, 0:width], 0.5, Pq.dq_r2[st][:, 0:width], OP.mult, OP.add)
        v.drain()
        v.tensor_tensor(Pq.dq_q[:, 0:width], Pq.dq_mag[:, 0:width], Pq.dq_fb[st][:, 0:width], OP.mult)
        v.drain()
        v.tensor_tensor(Pq.dq_sv[st][:, 0:width], Pq.dq_q[:, 0:width], scl[:, 0:width], OP.mult)
        v.drain()
        ctx.inc(v.nop(), "v", 1)
    blk.vector(vv)
    dqs["v" + str(st)] = ctx.cnt["v"]

def _new_dqs():
    return {"g0": 0, "g1": 0, "s0": 0, "s1": 0, "v0": 0, "v1": 0,
            "svfree_v0": 0, "svfree_v1": 0, "svfree_sg0": 0, "svfree_sg1": 0,
            "svfree_t0": 0, "svfree_t1": 0}

def build_stageC(scratch_out=False, n_ntiles=128, do_c0=True, do_c2=True):
    nc = bass.Bass("TRN2", target_bir_lowering=False)
    dram = {}
    dram["xn1T_d"] = nc.dram_tensor("xn1T_d", [D, T], BF16, kind="ExternalInput").ap()
    dram["x1_d"] = nc.dram_tensor("x1_d", [T, D], F32, kind="ExternalInput").ap()
    dram["s1_d"] = nc.dram_tensor("s1_d", [T, D], F32, kind="ExternalInput").ap()
    dram["Pg"] = nc.dram_tensor("Pg", [1024, 16384], I32, kind="ExternalInput").ap()
    dram["Pu"] = nc.dram_tensor("Pu", [1024, 16384], I32, kind="ExternalInput").ap()
    dram["Pd"] = nc.dram_tensor("Pd", [8192, 2048], I32, kind="ExternalInput").ap()
    dram["sgX"] = nc.dram_tensor("sgX", [128, 8, 16384], F32, kind="ExternalInput").ap()
    dram["suX"] = nc.dram_tensor("suX", [128, 8, 16384], F32, kind="ExternalInput").ap()
    dram["sdT"] = nc.dram_tensor("sdT", [512, 2048], F32, kind="ExternalInput").ap()
    sk = "ExternalOutput" if scratch_out else "Internal"
    dram["Wd_d"] = nc.dram_tensor("Wd_d", [16384, D], BF16, kind=sk).ap()
    dram["inter_d"] = nc.dram_tensor("inter_d", [16384, T], BF16, kind=sk).ap()
    dram["out_d"] = nc.dram_tensor("out_d", [T, D], F32, kind="ExternalOutput").ap()

    with ExitStack() as stack:
        ctx = Ctx(nc, stack)
        blk = stack.enter_context(nc.Block())
        _stageC_emit(nc, ctx, blk, stack, dram, n_ntiles, do_c0, do_c2)
    return nc

def _stageC_emit(nc, ctx, blk, stack0, dram, n_ntiles=128, do_c0=True, do_c2=True):
    with ExitStack() as stack:
        for s_ in ("lc", "lx", "sg", "t", "v", "s", "g", "w0", "w1"):
            ctx.sem(s_)

        if do_c0:
            with ExitStack() as sc:
                DQ = _alloc_dequant(nc, sc, 2048, "_c0")
                pk = sc.enter_context(nc.sbuf_tensor("pk_c0", [128, 2048], I32))
                scl = sc.enter_context(nc.sbuf_tensor("scl_c0", [128, 2048], BF16))
                sclf = sc.enter_context(nc.sbuf_tensor("sclf_c0", [128, 2048], F32))
                _emit_dq_init(ctx, blk, DQ)
                dqs = _new_dqs()
                for hs in range(16):
                    for hh in range(4):
                        def ld0(e, hs=hs, hh=hh):
                            ctx.wait(e, "lc")
                            ctx.wait(e, "g", max(dqs["g0"], dqs["g1"]))
                            src = dram["Pd"].rearrange("(bb q) h -> q bb h", q=128)[:, 16 * hh:16 * (hh + 1),
                                                                                   128 * hs:128 * (hs + 1)]
                            ctx.inc(e.dma_start(out=pk[:].rearrange("p (bb h) -> p bb h", bb=16), in_=src), "lc", 16)
                            for gg in range(8):
                                s2 = bass.AP(tensor=dram["sdT"].tensor,
                                             offset=(128 * hh + gg) * 2048 + 128 * hs,
                                             ap=[[0, 16], [8 * 2048, 16], [1, 128]])
                                dst = strided_ap(sclf[16 * gg:16 * (gg + 1), :], 0, [[128, 16], [1, 128]])
                                ctx.inc(e.dma_start(out=dst, in_=s2), "lc", 16)
                        blk.sync(ld0)
                        c0a = ctx.cnt["lc"]
                        def cv0(v, c0a=c0a, w_v=max(dqs["v0"], dqs["v1"])):
                            ctx.wait(v, "lc", c0a)
                            ctx.wait(v, "v", w_v)
                            v.tensor_copy(scl[:], sclf[:])
                            v.drain(); ctx.inc(v.nop(), "v", 1)
                        blk.vector(cv0)
                        for stream in range(2):
                            _emit_dequant_stream(nc, ctx, blk, DQ, pk, 4 * stream, scl, 2048, ctx.cnt["lc"], dqs)
                            dqs["svfree_sg" + str(stream)] = ctx.cnt["sg"] + 32
                            def st(e, hs=hs, hh=hh, stream=stream, w_v=dqs["v" + str(stream)]):
                                ctx.wait(e, "sg")
                                ctx.wait(e, "v", w_v)
                                for qh in range(2):
                                    base = 64 * stream + 128 * qh + 256 * 16 * hh
                                    out = bass.AP(tensor=dram["Wd_d"].tensor, offset=base * D + 128 * hs,
                                                  ap=[[D, 64], [256 * D, 16], [1, 128]])
                                    srcv = DQ.dq_sv[stream][64 * qh:64 * (qh + 1), :].rearrange("p (bb h) -> p bb h", bb=16)
                                    ctx.inc(e.dma_start(out=out, in_=srcv), "sg", 16)
                            blk.gpsimd(st)
        if n_ntiles == 0 and not do_c2:
            def fin0(e):
                ctx.wait(e, "sg"); ctx.wait(e, "lc")
            blk.gpsimd(fin0)
            return

        with ExitStack() as sc:
            DQ = _alloc_dequant(nc, sc, 1024, "_c1")
            xt = sc.enter_context(nc.sbuf_tensor("xt_c1", [128, 16 * T], BF16))
            pkx = [sc.enter_context(nc.sbuf_tensor(f"pkx_c1{i}", [128, 1024], I32)) for i in range(2)]
            sclf = [sc.enter_context(nc.sbuf_tensor(f"sclf_c1{i}", [128, 1024], F32)) for i in range(2)]
            wg = [sc.enter_context(nc.sbuf_tensor(f"wg_c1{i}", [128, 16 * 128], BF16)) for i in range(2)]
            wu = [sc.enter_context(nc.sbuf_tensor(f"wu_c1{i}", [128, 16 * 128], BF16)) for i in range(2)]
            itb = [sc.enter_context(nc.sbuf_tensor(f"itb_c1{i}", [128, 512], BF16)) for i in range(2)]
            gel = [sc.enter_context(nc.sbuf_tensor(f"gel_c1{i}", [128, 512], BF16)) for i in range(2)]
            uel = [sc.enter_context(nc.sbuf_tensor(f"uel_c1{i}", [128, 512], BF16)) for i in range(2)]
            psg = [sc.enter_context(nc.psum_tensor(f"psg_c1{i}", [128, 512], F32)) for i in range(4)]
            psu = [sc.enter_context(nc.psum_tensor(f"psu_c1{i}", [128, 512], F32)) for i in range(4)]
            _emit_dq_init(ctx, blk, DQ)
            dqs = _new_dqs()
            for s_ in ("i0", "i1"):
                ctx.sem(s_)
            def ldx(e):
                for s_ in ("sg", "v", "s", "g", "t", "lc"):
                    ctx.wait(e, s_)
                ctx.inc(e.dma_start(out=xt[:].rearrange("p (kt t) -> p kt t", kt=16),
                                    in_=dram["xn1T_d"].rearrange("(kt p) t -> p kt t", p=128)), "lx", 16)
            blk.sync(ldx)
            lxc = ctx.cnt["lx"]
            mm_done = [0, 0]
            scat_done = [0, 0]
            ge_done = {}
            dq_done_v = {}
            gst_cnt = {}
            def emit_decode(nt):
                par = nt % 2
                for wnm, scn, wdst in (("Pg", "sgX", wg[par]), ("Pu", "suX", wu[par])):
                    def ld(e, nt=nt, par=par, wnm=wnm, scn=scn,
                           w_v=max(dqs["v0"], dqs["v1"])):
                        ctx.wait(e, "lc")
                        ctx.wait(e, "g", max(dqs["g0"], dqs["g1"]))
                        ctx.wait(e, "v", w_v)
                        src = dram[wnm].rearrange("(bb q) n -> q bb n", q=128)[:, :, 128 * nt:128 * (nt + 1)]
                        ctx.inc(e.dma_start(out=pkx[par][:].rearrange("p (bb n) -> p bb n", bb=8), in_=src), "lc", 16)
                        ctx.inc(e.dma_start(out=sclf[par][:].rearrange("p (bb n) -> p bb n", bb=8),
                                            in_=dram[scn][:, :, 128 * nt:128 * (nt + 1)]), "lc", 16)
                    blk.sync(ld)
                    for stream in range(2):
                        dqs["svfree_v" + str(stream)] = scat_done[stream]
                        _emit_dequant_stream(nc, ctx, blk, DQ, pkx[par], 4 * stream, sclf[par], 1024, ctx.cnt["lc"], dqs)
                        def scat(v, wdst=wdst, stream=stream, par=par, nt=nt, w_v=dqs["v" + str(stream)],
                                 w_t=mm_done[par]):
                            ctx.wait(v, "v", w_v)
                            if nt >= 2:
                                ctx.wait(v, "t", w_t)
                            for qh in range(2):
                                dst = strided_ap(wdst[64 * stream:64 * (stream + 1), :], 128 * qh,
                                                 [[256, 8], [1, 128]])
                                v.tensor_copy(dst, DQ.dq_sv[stream][64 * qh:64 * (qh + 1), :].rearrange("p (bb n) -> p bb n", bb=8))
                            v.drain(); ctx.inc(v.nop(), "v", 1)
                        blk.vector(scat)
                        scat_done[stream] = ctx.cnt["v"]
                dq_done_v[nt] = ctx.cnt["v"]
            def emit_tcloop(nt):
                par = nt % 2
                for tc in range(8):
                    pp = tc % 4
                    pb = tc % 2
                    def gmm(t, nt=nt, tc=tc, par=par, pp=pp, lxc=lxc, dq_v=dq_done_v[nt]):
                        if tc == 0:
                            ctx.wait(t, "v", dq_v)
                            ctx.wait(t, "lx", lxc)
                        if (nt, tc - 4) in ge_done:
                            ctx.wait(t, "s", ge_done[(nt, tc - 4)][0])
                        elif nt > 0 and tc < 4:
                            ctx.wait(t, "s", ge_done[(nt - 1, 4 + tc)][0])
                        for kt in range(16):
                            _li = t.matmul(psg[pp][:], wg[par][:, 128 * kt:128 * (kt + 1)],
                                           xt[:, T * kt + 512 * tc:T * kt + 512 * (tc + 1)],
                                           start=(kt == 0), stop=(kt == 15))
                        for kt in range(16):
                            _li = t.matmul(psu[pp][:], wu[par][:, 128 * kt:128 * (kt + 1)],
                                           xt[:, T * kt + 512 * tc:T * kt + 512 * (tc + 1)],
                                           start=(kt == 0), stop=(kt == 15))
                        ctx.inc(_li, "t", 1)
                    blk.tensor(gmm)
                    t_g = ctx.cnt["t"]
                    def gev(s, nt=nt, tc=tc, pp=pp, pb=pb, t_g=t_g,
                            w_v=(ge_done.get((nt, tc - 2)) or ge_done.get((nt - 1, 6 + tc)) or (0, 0))[1]):
                        ctx.wait(s, "t", t_g)
                        if w_v:
                            ctx.wait(s, "v", w_v)
                        s.activation(gel[pb][:], psg[pp][:], AF.Gelu_apprx_tanh)
                        s.activation(uel[pb][:], psu[pp][:], AF.Copy)
                        s.drain(); ctx.inc(s.nop(), "s", 1)
                    blk.scalar(gev)
                    s_g = ctx.cnt["s"]
                    def gml(v, nt=nt, tc=tc, pp=pp, pb=pb, s_g=s_g,
                            w_i=gst_cnt.get((nt, tc - 2)) or gst_cnt.get((nt - 1, 6 + tc))):
                        ctx.wait(v, "s", s_g)
                        if w_i is not None:
                            ctx.wait(v, f"i{pb}", w_i)
                        v.tensor_tensor(itb[pb][:], gel[pb][:], uel[pb][:], OP.mult)
                        v.drain(); ctx.inc(v.nop(), "v", 1)
                    blk.vector(gml)
                    ge_done[(nt, tc)] = (ctx.cnt["s"], ctx.cnt["v"])
                    v_m = ctx.cnt["v"]
                    def gst(e, nt=nt, tc=tc, pb=pb, v_m=v_m, w_i=ctx.cnt[f"i{pb}"]):
                        ctx.wait(e, f"i{pb}", w_i)
                        ctx.wait(e, "v", v_m)
                        ctx.inc(e.dma_start(out=dram["inter_d"][128 * nt:128 * (nt + 1), 512 * tc:512 * (tc + 1)],
                                            in_=itb[pb][:]), f"i{pb}", 16)
                    blk.gpsimd(gst)
                    gst_cnt[(nt, tc)] = ctx.cnt[f"i{pb}"]
                mm_done[par] = ctx.cnt["t"]
            if n_ntiles > 0:
                emit_decode(0)
            if n_ntiles > 1:
                emit_decode(1)
            for nt in range(n_ntiles):
                emit_tcloop(nt)
                if nt + 2 < n_ntiles:
                    emit_decode(nt + 2)
        if not do_c2:
            def fin1(e):
                ctx.wait(e, "sg"); ctx.wait(e, "lc"); ctx.wait(e, "lx")
                for s_ in ("i0", "i1"):
                    ctx.wait(e, s_)
            blk.gpsimd(fin1)
            return

        with ExitStack() as sc:
            isl = sc.enter_context(nc.sbuf_tensor("isl_c2", [128, 128 * 512], BF16))
            wds = [sc.enter_context(nc.sbuf_tensor(f"wds{i}_c2", [128, 8 * 512], BF16)) for i in range(2)]
            gfb = sc.enter_context(nc.sbuf_tensor("gfb_c2", [128, 512], F32))
            x1b = sc.enter_context(nc.sbuf_tensor("x1b_c2", [128, 512], F32))
            s1b = sc.enter_context(nc.sbuf_tensor("s1b_c2", [128, 512], F32))
            x2b = sc.enter_context(nc.sbuf_tensor("x2b_c2", [128, 512], F32))
            psd = [sc.enter_context(nc.psum_tensor(f"psd{i}_c2", [128, 512], F32)) for i in range(4)]
            wd_free = [0, 0]
            for tc in range(8):
                def ldi(e, tc=tc):
                    ctx.wait(e, "lx")
                    ctx.wait(e, "t")
                    if tc == 0:
                        for s_ in ("sg", "v", "s", "g", "lc", "i0", "i1"):
                            ctx.wait(e, s_)
                    ctx.inc(e.dma_start(out=isl[:].rearrange("p (nt t) -> p nt t", nt=128),
                                        in_=dram["inter_d"].rearrange("(nt p) t -> p nt t", p=128)[:, :, 512 * tc:512 * (tc + 1)]),
                            "lx", 16)
                blk.sync(ldi)
                lxc2 = ctx.cnt["lx"]
                for hc in range(4):
                    for ng in range(16):
                        par = ng % 2
                        def lwd(e, tc=tc, hc=hc, ng=ng, par=par):
                            ctx.wait(e, f"w{par}")
                            ctx.wait(e, "t", wd_free[par])
                            src = dram["Wd_d"].rearrange("(nt p) h -> p nt h", p=128)[:, 8 * ng:8 * (ng + 1),
                                                                                     512 * hc:512 * (hc + 1)]
                            ctx.inc(e.dma_start(out=wds[par][:].rearrange("p (nt h) -> p nt h", nt=8),
                                                in_=src), f"w{par}", 16)
                        blk.sync(lwd)
                        wcn = ctx.cnt[f"w{par}"]
                        def dmm(t, tc=tc, hc=hc, ng=ng, par=par, wcn=wcn, lxc2=lxc2):
                            ctx.wait(t, f"w{par}", wcn)
                            ctx.wait(t, "lx", lxc2)
                            if ng == 0:
                                ctx.wait(t, "s")
                            for tt4 in range(4):
                                for j in range(8):
                                    nt = 8 * ng + j
                                    _li = t.matmul(psd[tt4][:], isl[:, 512 * nt + 128 * tt4:512 * nt + 128 * (tt4 + 1)],
                                                   wds[par][:, 512 * j:512 * (j + 1)],
                                                   start=(ng == 0 and j == 0), stop=(ng == 15 and j == 7))
                            ctx.inc(_li, "t", 1)
                        blk.tensor(dmm)
                        wd_free[par] = ctx.cnt["t"]
                    for tt4 in range(4):
                        def fnl(e, tc=tc, hc=hc, tt4=tt4):
                            ctx.wait(e, "lc")
                            ctx.wait(e, "v")
                            r0 = 512 * tc + 128 * tt4
                            ctx.inc(e.dma_start(out=x1b[:], in_=dram["x1_d"][r0:r0 + 128, 512 * hc:512 * (hc + 1)]), "lc", 16)
                            ctx.inc(e.dma_start(out=s1b[:], in_=dram["s1_d"][r0:r0 + 128, 512 * hc:512 * (hc + 1)]), "lc", 16)
                        blk.scalar(fnl)
                        lcc = ctx.cnt["lc"]
                        def fns(s, tc=tc, hc=hc, tt4=tt4):
                            ctx.wait(s, "t")
                            ctx.wait(s, "v")
                            s.activation(gfb[:], psd[tt4][:], AF.Silu)
                            s.drain(); ctx.inc(s.nop(), "s", 1)
                        blk.scalar(fns)
                        def fnv(v, tc=tc, hc=hc, tt4=tt4, lcc=lcc):
                            ctx.wait(v, "s")
                            ctx.wait(v, "lc", lcc)
                            ctx.wait(v, "sg")
                            v.tensor_tensor(x2b[:], s1b[:], gfb[:], OP.mult)
                            v.drain()
                            v.tensor_tensor(x2b[:], x2b[:], x1b[:], OP.add)
                            v.drain(); ctx.inc(v.nop(), "v", 1)
                        blk.vector(fnv)
                        def fst(e, tc=tc, hc=hc, tt4=tt4):
                            ctx.wait(e, "sg")
                            ctx.wait(e, "v")
                            r0 = 512 * tc + 128 * tt4
                            ctx.inc(e.dma_start(out=dram["out_d"][r0:r0 + 128, 512 * hc:512 * (hc + 1)],
                                                in_=x2b[:]), "sg", 16)
                        blk.gpsimd(fst)
        def fin(e):
            ctx.wait(e, "sg"); ctx.wait(e, "lc"); ctx.wait(e, "lx")
        blk.gpsimd(fin)


ALL_SEMS = ("ls", "lc", "sg", "t", "v", "s", "g", "w0", "w1", "q0", "q1", "lh", "lx")

def _all_barrier(ctx, blk):
    for en in ("tensor", "vector", "scalar", "sync", "gpsimd"):
        def w(e):
            for s_ in ALL_SEMS:
                ctx.wait(e, s_)
        getattr(blk, en)(w)

def build_fused(dumps=()):
    nc = bass.Bass("TRN2", target_bir_lowering=False)
    w_ins, w_outs = {}, {}
    for name, r, c in STAGEW_WEIGHTS:
        w_ins[name] = nc.dram_tensor(name, [r, c], F32, kind="ExternalInput").ap()
        w_outs[name] = nc.dram_tensor(name + "_b", [r, c], BF16, kind="Internal").ap()
    dram = {}
    dram["x0"] = nc.dram_tensor("x0", [T, D], F32, kind="ExternalInput").ap()
    for name, sh in STAGEA_WB:
        dram[name] = w_outs[name]
    dram["Wout"] = w_outs["Wout"]
    for name, sh in STAGEA_SMALL_F32:
        dram[name] = nc.dram_tensor(name, sh, F32, kind="ExternalInput").ap()
    for name, _, sh in STAGEA_ROWS:
        dram[name] = nc.dram_tensor(name, sh, F32, kind="ExternalInput").ap()
    for name, sh in STAGEA_CONST_BF:
        dram[name] = nc.dram_tensor(name, sh, BF16, kind="ExternalInput").ap()
    for name, sh, dt in STAGEA_SCRATCH:
        dram[name] = nc.dram_tensor(name, sh, dt, kind="Internal").ap()
    for name, sh, dt in STAGEA_OUT:
        dram[name] = nc.dram_tensor(name, sh, dt, kind="Internal").ap()
    for name in dumps:
        dt = BF16 if name in ("xn", "og") else F32
        dram["dump_" + name] = nc.dram_tensor("dump_" + name, [T, 2048], dt, kind="ExternalOutput").ap()
    dram["bout_row"] = nc.dram_tensor("bout_row", [1, 2048], F32, kind="ExternalInput").ap()
    dram["oT_d"] = nc.dram_tensor("oT_d", [D, T], BF16, kind="Internal").ap()
    dram["s1_d"] = nc.dram_tensor("s1_d", [T, D], F32, kind="Internal").ap()
    for name, sh in (("Pg", [1024, 16384]), ("Pu", [1024, 16384]), ("Pd", [8192, 2048])):
        dram[name] = nc.dram_tensor(name, sh, I32, kind="ExternalInput").ap()
    for name, sh in (("sgX", [128, 8, 16384]), ("suX", [128, 8, 16384]), ("sdT", [512, 2048])):
        dram[name] = nc.dram_tensor(name, sh, F32, kind="ExternalInput").ap()
    dram["Wd_d"] = nc.dram_tensor("Wd_d", [16384, D], BF16, kind="Internal").ap()
    dram["inter_d"] = nc.dram_tensor("inter_d", [16384, T], BF16, kind="Internal").ap()
    dram["out_d"] = nc.dram_tensor("out_d", [T, D], F32, kind="ExternalOutput").ap()

    with ExitStack() as stack:
        ctx = Ctx(nc, stack)
        blk = stack.enter_context(nc.Block())
        for s_ in ALL_SEMS:
            ctx.sem(s_)
        _stageW_emit(nc, ctx, blk, stack, w_ins, w_outs)
        _all_barrier(ctx, blk)
        _stageA_emit(nc, ctx, blk, stack, dram, NCH, dumps, 'tail')
        _all_barrier(ctx, blk)
        _stageB_emit(nc, ctx, blk, stack, dram, 16)
        _all_barrier(ctx, blk)
        _stageC_emit(nc, ctx, blk, stack, dram, 128, True, True)
        def fin(e):
            for s_ in ALL_SEMS:
                ctx.wait(e, s_)
        blk.gpsimd(fin)
    return nc

def fused_in_map(inputs, g, core):
    m = {name: g[name] for name, _, _ in STAGEW_WEIGHTS}
    m["x0"] = np.ascontiguousarray(np.asarray(inputs["L0_attn_hidden_states"][core], np.float32))
    for name, sh in STAGEA_SMALL_F32:
        m[name] = g[name]
    for name, _, sh in STAGEA_ROWS:
        m[name] = g[name]
    for name, sh in STAGEA_CONST_BF:
        m[name] = g[name] if name != "ProwShift" else np.ascontiguousarray(g["ProwShift"].reshape(960, 128))
    m["bout_row"] = g["bout_row"]
    for name in ("Pg", "Pu", "Pd", "sgX", "suX", "sdT"):
        m[name] = g[name]
    return m


def kernel_unfused(**inputs):
    n_cores = 8
    g = host_prep(inputs)
    g["resetmask"] = host_resetmask()
    g["sgX"] = host_scale_expand_gu(g["sgT"])
    g["suX"] = host_scale_expand_gu(g["suT"])
    x0_all = np.asarray(inputs["L0_attn_hidden_states"], np.float32)

    ncW = build_stageW()
    mapW = {name: g[name] for name, _, _ in STAGEW_WEIGHTS}
    resW = run_bass_kernel_spmd(ncW, [mapW] * n_cores, core_ids=list(range(n_cores))).results
    wb = [{name + "_b": r[name + "_b"] for name, _, _ in STAGEW_WEIGHTS} for r in resW]

    ncA = build_stageA()
    mapsA = []
    for c in range(n_cores):
        m = {"x0": np.ascontiguousarray(x0_all[c])}
        for name, sh in STAGEA_WB:
            m[name + "_b"] = wb[c][name + "_b"]
        for name, sh in STAGEA_SMALL_F32:
            m[name] = g[name]
        for name, row, sh in STAGEA_ROWS:
            m[name] = g[name]
        for name, sh in STAGEA_CONST_BF:
            m[name] = g[name] if name != "ProwShift" else np.ascontiguousarray(g["ProwShift"].reshape(960, 128))
        mapsA.append(m)
    resA = run_bass_kernel_spmd(ncA, mapsA, core_ids=list(range(n_cores))).results

    ncB = build_stageB()
    mapsB = []
    for c in range(n_cores):
        mapsB.append({"qT_d": resA[c]["qT_d"], "kT_d": resA[c]["kT_d"],
                      "v1tok_d": resA[c]["v1tok_d"], "Wout_b": wb[c]["Wout_b"],
                      "bout_row": g["bout_row"], "ones_row_f32": g["ones_row_f32"]})
    resB = run_bass_kernel_spmd(ncB, mapsB, core_ids=list(range(n_cores))).results

    ncC = build_stageC()
    mapsC = []
    for c in range(n_cores):
        mapsC.append({"xn1T_d": resA[c]["xn1T_d"], "x1_d": resA[c]["x1_d"],
                      "s1_d": resB[c]["s1_d"], "Pg": g["Pg"], "Pu": g["Pu"], "Pd": g["Pd"],
                      "sgX": g["sgX"], "suX": g["suX"], "sdT": g["sdT"]})
    resC = run_bass_kernel_spmd(ncC, mapsC, core_ids=list(range(n_cores))).results

    out = np.stack([np.asarray(resC[c]["out_d"], np.float32) for c in range(n_cores)], axis=0)
    return out


def kernel(**inputs):
    n_cores = 8
    g = host_prep(inputs)
    g["resetmask"] = host_resetmask()
    g["sgX"] = host_scale_expand_gu(g["sgT"])
    g["suX"] = host_scale_expand_gu(g["suT"])
    nc = build_fused()
    maps = [fused_in_map(inputs, g, c) for c in range(n_cores)]
    res = run_bass_kernel_spmd(nc, maps, core_ids=list(range(n_cores))).results
    return np.stack([np.asarray(res[c]["out_d"], np.float32) for c in range(n_cores)], axis=0)
